# version 15 (final)
# baseline (speedup 1.0000x reference)
import numpy as np
from contextlib import ExitStack
import concourse.bass as bass
import concourse.mybir as mybir
from concourse.bass_utils import run_bass_kernel_spmd

F32 = mybir.dt.float32
BF16 = mybir.dt.bfloat16
AF = mybir.ActivationFunctionType
ALU = mybir.AluOpType
AXX = mybir.AxisListType.X

H, P, N, KC = 32, 64, 32, 4
D = 2048
DC = 16
CONV = 2112
NCC = 17
PROJ = 4192
INTER = 8192
IC = 64
T_OWN = 2048
T_HALO = 128
T_ALL = 2176
NCH = 16
EPS_LN = 1e-5
EPS_RMS = 1e-6
MASK_NEG = -1e30

TT_A = 384
TT_B = 384
TT_C = 256
NPAN = 4


def _tiles(total, tt):
    out, t0 = [], 0
    while t0 < total:
        out.append((t0, min(tt, total - t0)))
        t0 += out[-1][1]
    return out


class Builder:
    ENGINES = ("pe", "dve", "act", "sp", "pool")

    def __init__(self, nc):
        self.nc = nc
        self.prog = {e: [] for e in self.ENGINES}
        self.cnt = {e: 0 for e in self.ENGINES}
        self.implied = {e: {} for e in self.ENGINES}
        self.drain_cnt = {e: 0 for e in self.ENGINES}
        self.icount = {e: 0 for e in self.ENGINES}
        self.bufs = {}
        self.sems = {}
        self.dsem = {}
        self.dcnt = {}
        self.cc_sem = None
        self.es = None

    def make_sems(self, es):
        self.es = es
        for e in self.ENGINES:
            self.sems[e] = es.enter_context(self.nc.semaphore(f"s_{e}"))
        self.cc_sem = es.enter_context(self.nc.semaphore("s_cc"))

    def _slot_sem(self, slot):
        if slot not in self.dsem:
            self.dsem[slot] = self.es.enter_context(
                self.nc.semaphore(f"d_{len(self.dsem)}"))
            self.dcnt[slot] = 0
        return self.dsem[slot]

    def _wait(self, eng, src, val):
        if val <= 0:
            return
        if self.implied[eng].get(src, 0) >= val:
            return
        if src == eng and val >= self.cnt.get(eng, 0) + 1:
            raise RuntimeError("self-wait on future count")
        self.implied[eng][src] = val
        sem = self.sems[src] if src in self.sems else self.dsem[src]
        self.prog[eng].append(lambda E, sem=sem, val=val: E.wait_ge(sem, val))

    def _dep(self, eng, key, write):
        rec = self.bufs.get(key)
        if rec is None:
            return
        deps = []
        if rec['w'] is not None:
            deps.append(rec['w'])
        if write:
            deps.extend((e, c, i) for e, (c, i) in rec['r'].items())
        for (e, c, i) in deps:
            if c <= 0:
                raise RuntimeError(f"dep on un-fenced work of {e} for {key}")
            self._wait(eng, e, c)

    def _record(self, eng, reads, writes, my_icnt):
        for k in reads:
            rec = self.bufs.setdefault(k, {'w': None, 'r': {}})
            rec['r'][eng] = (self.cnt[eng], my_icnt)
        for k in writes:
            rec = self.bufs.setdefault(k, {'w': None, 'r': {}})
            rec['w'] = (eng, self.cnt[eng], my_icnt)
            rec['r'] = {}

    def op(self, eng, fn, reads=(), writes=()):
        for k in reads:
            self._dep(eng, k, False)
        for k in writes:
            self._dep(eng, k, True)
        self.icount[eng] += 1
        my_icnt = self.icount[eng]
        self.cnt[eng] += 1
        sem = self.sems[eng]
        self.prog[eng].append(lambda E, fn=fn, sem=sem: fn(E).then_inc(sem, 1))
        self._record(eng, reads, writes, my_icnt)
        return self.cnt[eng]

    def mms(self, fns, reads=(), writes=(), claim=(), publish=()):
        eng = "pe"
        for k in reads:
            self._dep(eng, k, False)
        for k in claim:
            self._dep(eng, k, True)
        for k in writes:
            self._dep(eng, k, True)
        self.icount[eng] += len(fns)
        my_icnt = self.icount[eng]
        self.cnt[eng] += 1
        sem = self.sems[eng]
        for fn in fns[:-1]:
            self.prog[eng].append(lambda E, fn=fn: fn(E))
        self.prog[eng].append(lambda E, fn=fns[-1], sem=sem: fn(E).then_inc(sem, 1))
        self._record(eng, reads, tuple(writes) + tuple(claim) + tuple(publish), my_icnt)
        return self.cnt[eng]

    def dma(self, queue, out, in_, reads=(), writes=(), slot=None):
        if slot is None:
            slot = ("w:" + writes[0]) if writes else ("r:" + reads[0])
        sem = self._slot_sem(slot)
        if self.dcnt[slot] > 0:
            self._wait(queue, slot, self.dcnt[slot])
        for k in reads:
            self._dep(queue, k, False)
        for k in writes:
            self._dep(queue, k, True)
        self.icount[queue] += 1
        my_icnt = self.icount[queue]
        self.dcnt[slot] += 16
        cnt = self.dcnt[slot]
        self.prog[queue].append(
            lambda E, out=out, in_=in_, sem=sem, :
                E.dma_start(out=out, in_=in_).then_inc(sem, 16))
        for k in reads:
            rec = self.bufs.setdefault(k, {'w': None, 'r': {}})
            rec['r'][slot] = (cnt, my_icnt)
        for k in writes:
            rec = self.bufs.setdefault(k, {'w': None, 'r': {}})
            rec['w'] = (slot, cnt, my_icnt)
            rec['r'] = {}
        return cnt

    def barrier_all(self):
        snap = dict(self.cnt)
        dsnap = dict(self.dcnt)
        for e in self.ENGINES:
            for e2 in self.ENGINES:
                self._wait(e, e2, snap[e2])
            for s, v in dsnap.items():
                self._wait(e, s, v)
        self.bufs = {}

    def finalize(self, block):
        progs = self.prog

        @block.tensor
        def _(E):
            for f in progs["pe"]:
                f(E)

        @block.vector
        def _(E):
            for f in progs["dve"]:
                f(E)

        @block.scalar
        def _(E):
            for f in progs["act"]:
                f(E)

        @block.sync
        def _(E):
            for f in progs["sp"]:
                f(E)

        @block.gpsimd
        def _(E):
            for f in progs["pool"]:
                f(E)


class Arena:
    def __init__(self, nc, es, name, cols):
        self.t = es.enter_context(nc.sbuf_tensor(name, [128, cols], F32))
        self.off = 0
        self.marks = []

    def alloc(self, cols):
        a = self.off
        self.off += cols
        assert self.off <= self.t.shape[1], f"SBUF arena overflow: {self.off}"
        return a

    def v(self, off, cols, p0=0, p1=128):
        return self.t[p0:p1, off:off + cols]

    def push(self):
        self.marks.append(self.off)

    def pop(self):
        self.off = self.marks.pop()


def build_program(debug=False, stages=("A", "B", "CC", "C1", "C2")):
    nc = bass.Bass("TRN2", target_bir_lowering=False, debug=False)
    es = ExitStack()

    def din(name, shape):
        return nc.dram_tensor(name, shape, F32, kind="ExternalInput").ap()

    x_pre = din("x_pre", [T_ALL, D])
    WguT = din("WguT", [D, 2 * INTER])
    WdT = din("WdT", [INTER, D])
    W_in = din("W_in", [D, PROJ])
    W_out = din("W_out", [D, D])
    b_in_t = din("b_in_t", [128, 33])
    conv_w_t = din("conv_w_t", [128, NCC * 4])
    conv_b_t = din("conv_b_t", [128, NCC])
    d_rep_t = din("d_rep_t", [128, DC])
    d_rep_row = din("d_rep_row", [1, D])
    norm_w_t = din("norm_w_t", [128, DC])
    a_log = din("a_log", [32, 1])
    dt_bias_t = din("dt_bias_t", [32, 1])
    consts = din("consts", [128, 3 * 128 + 3])
    gates_t = din("gates_t", [128, 2])

    out_t = nc.dram_tensor("out", [T_OWN, D], F32, kind="ExternalOutput").ap()
    skind = "ExternalOutput" if debug else "Internal"
    x1_s = nc.dram_tensor("x1_s", [T_ALL, D], F32, kind=skind).ap()
    y0_s = nc.dram_tensor("y0_s", [T_OWN, D], F32, kind=skind).ap()
    gs_s = nc.dram_tensor("gs_s", [DC, 128, T_OWN], F32, kind=skind).ap()
    x2_s = nc.dram_tensor("x2_s", [T_OWN, D], F32, kind=skind).ap()
    WguT_h = nc.dram_tensor("WguT_h", [D, 2 * INTER], BF16).ap()
    WguT_l = nc.dram_tensor("WguT_l", [D, 2 * INTER], BF16).ap()
    WdT_h = nc.dram_tensor("WdT_h", [INTER, D], BF16).ap()
    WdT_l = nc.dram_tensor("WdT_l", [INTER, D], BF16).ap()
    W_in_h = nc.dram_tensor("W_in_h", [D, PROJ], BF16).ap()
    W_in_l = nc.dram_tensor("W_in_l", [D, PROJ], BF16).ap()
    W_out_h = nc.dram_tensor("W_out_h", [D, D], BF16).ap()
    W_out_l = nc.dram_tensor("W_out_l", [D, D], BF16).ap()
    cc_in = nc.dram_tensor("cc_in", [N, H * P], F32).ap()
    cc_out = nc.dram_tensor("cc_out", [N, H * P], F32).ap()
    hin_dbg = nc.dram_tensor("hin_dbg", [N, H * P], F32, kind=skind).ap() if debug else None
    if debug:
        d_stfin = nc.dram_tensor("d_stfin", [N, H * P], F32, kind=skind).ap()
        d_AS = nc.dram_tensor("d_AS", [128, 32], F32, kind=skind).ap()
        d_dtf = nc.dram_tensor("d_dtf", [32, TT_B], F32, kind=skind).ap()
        d_xs0 = nc.dram_tensor("d_xs0", [128, 128], F32, kind=skind).ap()
        d_b = nc.dram_tensor("d_b", [32, TT_B], F32, kind=skind).ap()
        d_c = nc.dram_tensor("d_c", [32, TT_B], F32, kind=skind).ap()
        d_g0 = nc.dram_tensor("d_g0", [128, 128], F32, kind=skind).ap()
        d_msb = nc.dram_tensor("d_msb", [128, 128], F32, kind=skind).ap()
        d_esb = nc.dram_tensor("d_esb", [128, 128], F32, kind=skind).ap()
        d_state = nc.dram_tensor("d_state", [N, H * P], F32, kind=skind).ap()
        d_wtm = nc.dram_tensor("d_wtm", [128, 32], F32, kind=skind).ap()
        d_db = nc.dram_tensor("d_db", [N, H * P], F32, kind=skind).ap()

    b = Builder(nc)
    b.make_sems(es)
    ar = Arena(nc, es, "arena", 53152)

    o_ident = ar.alloc(128)
    o_tri = ar.alloc(128)
    o_mask = ar.alloc(128)
    o_ones = ar.alloc(1)
    o_eps = ar.alloc(2)
    o_gates = ar.alloc(2)
    o_bin = ar.alloc(33)
    o_convw = ar.alloc(NCC * 4)
    o_convb = ar.alloc(NCC)
    o_drep = ar.alloc(DC)
    o_normw = ar.alloc(DC)
    o_alog = ar.alloc(1)
    o_dtb = ar.alloc(1)
    o_state = ar.alloc(H * P)
    o_hinit = ar.alloc(H * P)
    o_lc = ar.alloc(1)
    o_ctm = ar.alloc(NCH * 32)
    o_ftm = ar.alloc(NCH * 32)
    o_tail = ar.alloc(NCC * 3)
    o_drepb = ar.alloc(D)

    ident = ar.v(o_ident, 128)
    tri = ar.v(o_tri, 128)

    def ones_lhsT(m):
        return ar.t[0:1, o_ones:o_ones + 1].broadcast_to([1, m])

    def ones_col():
        return ar.t[:, o_ones:o_ones + 1]

    ps_big = es.enter_context(nc.psum_tensor("psbig", [128, 2048], F32))
    pb = [es.enter_context(nc.psum_tensor(f"pb{i}", [128, 512], F32)) for i in range(4)]

    def psv(j, c0, cols, p0=0, p1=128):
        if j < 4:
            return ps_big[p0:p1, j * 512 + c0: j * 512 + c0 + cols]
        return pb[j - 4][p0:p1, c0:c0 + cols]

    BK = [f"bank{j}" for j in range(8)]

    b.dma("sp", ar.v(o_ident, 387), consts[:, :],
          writes=["c_ident", "c_tri", "c_mask", "c_ones", "c_eps"])
    nc.const_aps.register(F32, EPS_LN, ar.t[:, o_eps:o_eps + 1])
    nc.const_aps.register(F32, EPS_RMS, ar.t[:, o_eps + 1:o_eps + 2])
    b.dma("sp", ar.v(o_gates, 2), gates_t[:, :], writes=["c_gates"])
    b.dma("sp", ar.v(o_bin, 33), b_in_t[:, :], writes=["c_bin"])
    b.dma("sp", ar.v(o_convw, NCC * 4), conv_w_t[:, :], writes=["c_convw"])
    b.dma("sp", ar.v(o_convb, NCC), conv_b_t[:, :], writes=["c_convb"])
    b.dma("sp", ar.v(o_drep, DC), d_rep_t[:, :], writes=["c_drep"])
    b.dma("sp", ar.v(o_normw, DC), norm_w_t[:, :], writes=["c_normw"])
    b.dma("sp", ar.v(o_alog, 1, 0, 32), a_log[:, :], writes=["c_alog"])
    b.dma("sp", ar.v(o_dtb, 1, 0, 32), dt_bias_t[:, :], writes=["c_dtb"])
    b.op("act", lambda E: E.activation(ar.v(o_alog, 1, 0, 32), ar.v(o_alog, 1, 0, 32), AF.Exp),
         reads=["c_alog"], writes=["c_alog"])
    b.op("dve", lambda E: E.tensor_scalar_mul(ar.v(o_alog, 1, 0, 32), ar.v(o_alog, 1, 0, 32), -1.0),
         reads=["c_alog"], writes=["c_alog"])
    b.op("dve", lambda E: E.memset(ar.v(o_state, H * P, 0, 32), 0.0), writes=["state"])
    b.op("dve", lambda E: E.memset(ar.v(o_lc, 1, 0, 32), 0.0), writes=["lc"])
    b.op("dve", lambda E: E.memset(ar.v(o_tail, NCC * 3), 0.0), writes=["tail"])
    b.dma("sp", ar.t[0:1, o_drepb:o_drepb + D], d_rep_row[:, :], writes=["dreprow"])
    for q in range(4):
        b.mms([lambda E, q=q: E.matmul(pb[3][:, 0:512], ones_lhsT(128),
                                       ar.t[0:1, o_drepb + q * 512:o_drepb + (q + 1) * 512],
                                       start=True, stop=True)],
              reads=["dreprow", "c_ones"], writes=["bank7"])
        b.op("dve", lambda E, q=q: E.tensor_copy(ar.v(o_drepb + q * 512, 512), pb[3][:, 0:512]),
             reads=["bank7"], writes=["drepb"])

    def emit_ln_chunk(o_xc, o_stat, o_sqt, kx, kst, ksq="lnsq"):
        xc = ar.v(o_xc, D)
        s_m = ar.t[:, o_stat:o_stat + 1]
        s_v = ar.t[:, o_stat + 1:o_stat + 2]
        s_t = ar.t[:, o_stat + 2:o_stat + 3]
        sq = ar.v(o_sqt, D)
        b.op("act", lambda E: E.activation(sq, xc, AF.Square, accum_out=s_v),
             reads=[kx], writes=[ksq, kst + "v"])
        b.op("dve", lambda E: E.reduce_sum(s_m, xc, axis=AXX),
             reads=[kx], writes=[kst + "m"])
        b.op("act", lambda E: E.mul(s_m, s_m, 1.0 / D),
             reads=[kst + "m"], writes=[kst + "m"])
        b.op("act", lambda E: E.mul(s_v, s_v, 1.0 / D),
             reads=[kst + "v"], writes=[kst + "v"])
        b.op("act", lambda E: E.activation(s_t, s_m, AF.Square),
             reads=[kst + "m"], writes=[kst + "t"])
        b.op("dve", lambda E: E.tensor_sub(s_v, s_v, s_t),
             reads=[kst + "v", kst + "t"], writes=[kst + "v"])
        b.op("act", lambda E: E.activation(s_v, s_v, AF.Sqrt, bias=EPS_LN),
             reads=[kst + "v", "c_eps"], writes=[kst + "v"])
        b.op("dve", lambda E: E.reciprocal(s_v, s_v),
             reads=[kst + "v"], writes=[kst + "v"])
        b.op("dve", lambda E: E.tensor_scalar(sq, xc, s_m, s_v,
                                              op0=ALU.subtract, op1=ALU.mult),
             reads=[kst + "m", kst + "v", kx], writes=[ksq])

    def emit_tp(src_ap, dst_ap, kin, kout, bank, c0, src_p, src_c):
        pvw = psv(bank, c0, src_p, 0, src_c)
        id_ap = ar.t[0:src_p, o_ident:o_ident + src_p]
        b.op("pe", lambda E: E.transpose(pvw, src_ap, id_ap),
             reads=[kin, "c_ident"], writes=[BK[bank]])
        b.op("dve", lambda E: E.tensor_copy(dst_ap, pvw),
             reads=[BK[bank]], writes=[kout])

    def emit_ffn_stage(t_total, src_ap_fn, dst_ap_fn, split_first):
        ar.push()
        o_xtm = ar.alloc(3 * D)
        o_xhh = ar.alloc(DC * TT_A // 2)
        o_xhl = ar.alloc(DC * TT_A // 2)
        o_hh = ar.alloc(IC * TT_A // 2)
        o_hl = ar.alloc(IC * TT_A // 2)
        NP = 3
        o_ph = ar.alloc(NP * 512)
        o_pl = ar.alloc(NP * 512)
        o_pf = ar.alloc(1024)
        o_gtmp = ar.alloc(8 * TT_A)
        o_sqt = o_gtmp
        o_ptmp = ar.alloc(TT_A)
        o_dn = o_ptmp
        o_stat = ar.alloc(12)
        pan_i = [0]

        def bfv(off_f32, col0_bf, ncol_bf):
            return ar.t[:, off_f32:off_f32 + 1].bitcast(BF16).tensor[
                :, off_f32 * 2 + col0_bf: off_f32 * 2 + col0_bf + ncol_bf] \
                if False else ar.t[:, off_f32: off_f32 + (col0_bf + ncol_bf + 1) // 2] \
                .bitcast(BF16)[:, col0_bf:col0_bf + ncol_bf]

        def panel(dram_f, dram_h, dram_l, wk, first, w=1024):
            pi = pan_i[0] % NP
            pan_i[0] += 1
            hi = ar.v(o_ph + pi * 512, (w + 1) // 2).bitcast(BF16)[:, :w]
            lo = ar.v(o_pl + pi * 512, (w + 1) // 2).bitcast(BF16)[:, :w]
            if first:
                pv = ar.v(o_pf, w)
                b.dma("sp", pv, dram_f, writes=["pf0"])
                b.op("act", lambda E, hi=hi, pv=pv: E.activation(hi, pv, AF.Copy),
                     reads=["pf0"], writes=[f"ph{pi}"])
                b.op("dve", lambda E, lo=lo, pv=pv, hi=hi:
                     E.tensor_tensor(lo, pv, hi, op=ALU.subtract),
                     reads=["pf0", f"ph{pi}"], writes=[f"pl{pi}"])
                b.dma("pool", dram_h, hi, reads=[f"ph{pi}"], writes=[wk], slot="wch")
                b.dma("pool", dram_l, lo, reads=[f"pl{pi}"], writes=[wk], slot="wcl")
            else:
                b.dma("sp", hi, dram_h, writes=[f"ph{pi}"], reads=[wk])
                b.dma("sp", lo, dram_l, writes=[f"pl{pi}"], reads=[wk])
            return hi, lo, pi

        def mm3(j, tt, whi, wlo, xhi, xlo, first, last):
            return [
                (lambda E, j=j, tt=tt, whi=whi, xhi=xhi, first=first:
                 E.matmul(psv(j, 0, tt), whi, xhi, start=first, stop=False)),
                (lambda E, j=j, tt=tt, whi=whi, xlo=xlo:
                 E.matmul(psv(j, 0, tt), whi, xlo, start=False, stop=False)),
                (lambda E, j=j, tt=tt, wlo=wlo, xhi=xhi, last=last:
                 E.matmul(psv(j, 0, tt), wlo, xhi, start=False, stop=last)),
            ]

        for ti, (t0, tt) in enumerate(_tiles(t_total, TT_A)):
            ntc = tt // 128
            for c in range(ntc):
                kx = f"xtm{c}"
                b.dma("pool", ar.v(o_xtm + c * D, D), src_ap_fn(t0 + c * 128, 128),
                      writes=[kx])
                emit_ln_chunk(o_xtm + c * D, o_stat + 4 * c, o_sqt, kx, f"ln{c}", ksq="scr")
                for f in range(DC):
                    pvw = psv(7, 0, 128)
                    b.op("pe", lambda E, f=f, c=c, pvw=pvw:
                         E.transpose(pvw, ar.v(o_sqt + f * 128, 128), ident),
                         reads=["scr", "c_ident"], writes=[BK[7]])
                    hi = ar.v(o_xhh + f * TT_A // 2, TT_A // 2).bitcast(BF16)[:, c * 128:c * 128 + 128]
                    lo = ar.v(o_xhl + f * TT_A // 2, TT_A // 2).bitcast(BF16)[:, c * 128:c * 128 + 128]
                    b.op("act", lambda E, hi=hi, pvw=pvw: E.activation(hi, pvw, AF.Copy),
                         reads=[BK[7]], writes=[f"xhh{f}"])
                    b.op("dve", lambda E, lo=lo, pvw=pvw, hi=hi:
                         E.tensor_tensor(lo, pvw, hi, op=ALU.subtract),
                         reads=[BK[7], f"xhh{f}"], writes=[f"xhl{f}"])
            for g in range(8):
                for mg in (g, 8 + g):
                    for k in range(DC):
                        whi, wlo, pi = panel(WguT[k * 128:(k + 1) * 128, mg * 1024:(mg + 1) * 1024],
                                             WguT_h[k * 128:(k + 1) * 128, mg * 1024:(mg + 1) * 1024],
                                             WguT_l[k * 128:(k + 1) * 128, mg * 1024:(mg + 1) * 1024],
                                             "splitWgu%d_%d" % (k, mg), split_first and ti == 0)
                        xhi = ar.v(o_xhh + k * TT_A // 2, TT_A // 2).bitcast(BF16)[:, 0:tt]
                        xlo = ar.v(o_xhl + k * TT_A // 2, TT_A // 2).bitcast(BF16)[:, 0:tt]
                        if k == 0:
                            for hf in range(2):
                                fns = []
                                for j in range(hf * 4, hf * 4 + 4):
                                    fns += mm3(j, tt, whi[:, j * 128:(j + 1) * 128],
                                               wlo[:, j * 128:(j + 1) * 128], xhi, xlo,
                                               True, False)
                                b.mms(fns, reads=[f"ph{pi}", f"pl{pi}", f"xhh{k}", f"xhl{k}"],
                                      claim=BK[hf * 4:hf * 4 + 4])
                        else:
                            fns = []
                            for j in range(8):
                                fns += mm3(j, tt, whi[:, j * 128:(j + 1) * 128],
                                           wlo[:, j * 128:(j + 1) * 128], xhi, xlo,
                                           False, k == DC - 1)
                            b.mms(fns, reads=[f"ph{pi}", f"pl{pi}", f"xhh{k}", f"xhl{k}"],
                                  publish=BK if k == DC - 1 else ())
                    for j in range(8):
                        if mg < 8:
                            gv = ar.v(o_gtmp + j * TT_A, tt)
                            b.op("act", lambda E, gv=gv, j=j, tt=tt:
                                 E.activation(gv, psv(j, 0, tt), AF.Silu),
                                 reads=[BK[j]], writes=["scr"])
                        else:
                            hj = (mg - 8) * 8 + j
                            pt = ar.v(o_ptmp, tt)
                            hhv = ar.v(o_hh + hj * TT_A // 2, TT_A // 2).bitcast(BF16)[:, 0:tt]
                            hlv = ar.v(o_hl + hj * TT_A // 2, TT_A // 2).bitcast(BF16)[:, 0:tt]
                            b.op("dve", lambda E, pt=pt, j=j, tt=tt:
                                 E.tensor_tensor(pt, ar.v(o_gtmp + j * TT_A, tt),
                                                 psv(j, 0, tt), op=ALU.mult),
                                 reads=[BK[j], "scr"], writes=["dnp"])
                            b.op("act", lambda E, hhv=hhv, pt=pt: E.activation(hhv, pt, AF.Copy),
                                 reads=["dnp"], writes=[f"hh{hj}"])
                            b.op("dve", lambda E, hlv=hlv, pt=pt, hhv=hhv:
                                 E.tensor_tensor(hlv, pt, hhv, op=ALU.subtract),
                                 reads=["dnp", f"hh{hj}"], writes=[f"hl{hj}"])
            for mg in range(2):
                for k in range(IC):
                    whi, wlo, pi = panel(WdT[k * 128:(k + 1) * 128, mg * 1024:(mg + 1) * 1024],
                                         WdT_h[k * 128:(k + 1) * 128, mg * 1024:(mg + 1) * 1024],
                                         WdT_l[k * 128:(k + 1) * 128, mg * 1024:(mg + 1) * 1024],
                                         "splitWd%d_%d" % (k, mg), split_first and ti == 0)
                    hhv = ar.v(o_hh + k * TT_A // 2, TT_A // 2).bitcast(BF16)[:, 0:tt]
                    hlv = ar.v(o_hl + k * TT_A // 2, TT_A // 2).bitcast(BF16)[:, 0:tt]
                    if k == 0:
                        for hf in range(2):
                            fns = []
                            for j in range(hf * 4, hf * 4 + 4):
                                fns += mm3(j, tt, whi[:, j * 128:(j + 1) * 128],
                                           wlo[:, j * 128:(j + 1) * 128], hhv, hlv,
                                           True, False)
                            b.mms(fns, reads=[f"ph{pi}", f"pl{pi}", f"hh{k}", f"hl{k}"],
                                  claim=BK[hf * 4:hf * 4 + 4])
                    else:
                        fns = []
                        for j in range(8):
                            fns += mm3(j, tt, whi[:, j * 128:(j + 1) * 128],
                                       wlo[:, j * 128:(j + 1) * 128], hhv, hlv,
                                       False, k == IC - 1)
                        b.mms(fns, reads=[f"ph{pi}", f"pl{pi}", f"hh{k}", f"hl{k}"],
                              publish=BK if k == IC - 1 else ())
                for j in range(8):
                    f = mg * 8 + j
                    dn = ar.v(o_dn, tt)
                    b.op("act", lambda E, dn=dn, j=j, tt=tt:
                         E.activation(dn, psv(j, 0, tt), AF.Copy, scale=0.5),
                         reads=[BK[j]], writes=["dnp"])
                    for c in range(ntc):
                        b.op("pe", lambda E, c=c, j=j:
                             E.transpose(psv(j, 0, 128), ar.v(o_dn + c * 128, 128), ident),
                             reads=["dnp", "c_ident"], writes=[BK[j]])
                        xsl = ar.v(o_xtm + c * D + f * 128, 128)
                        b.op("dve", lambda E, xsl=xsl, j=j:
                             E.tensor_add(xsl, xsl, psv(j, 0, 128)),
                             reads=[BK[j], f"xtm{c}"], writes=[f"xtm{c}"])
            for c in range(ntc):
                b.dma("pool", dst_ap_fn(t0 + c * 128, 128), ar.v(o_xtm + c * D, D),
                      reads=[f"xtm{c}"])
        ar.pop()

    if "A" in stages:
        emit_ffn_stage(T_ALL,
                       lambda t, n: x_pre[t:t + n, :],
                       lambda t, n: x1_s[t:t + n, :], True)
        b.barrier_all()

    if "B" in stages:
        ar.push()
        o_xtm = ar.alloc(D)
        o_xhh = ar.alloc(DC * TT_B // 2)
        o_xhl = ar.alloc(DC * TT_B // 2)
        NPB = 6
        o_pph = ar.alloc(NPB * 512)
        o_ppl = ar.alloc(NPB * 512)
        o_pfB = ar.alloc(1024)
        o_stat = ar.alloc(12)
        o_sqt = ar.alloc(D)
        o_hcat = ar.alloc(NCC * (TT_B + 3))
        o_xs = ar.alloc(NCC * TT_B)
        o_gsb = ar.alloc(2 * TT_B)
        o_p32 = ar.alloc(TT_B)
        o_bf0 = ar.alloc(TT_B)
        o_cf0 = ar.alloc(TT_B)
        o_ca1 = ar.alloc(TT_B)
        o_ca2 = ar.alloc(TT_B)
        o_dtraw = ar.alloc(TT_B)
        o_dts1 = ar.alloc(TT_B)
        o_dts2 = ar.alloc(TT_B)
        o_dtf = ar.alloc(TT_B)
        o_dtA = ar.alloc(TT_B)
        o_xstm = ar.alloc(D)
        o_dtAtm = ar.alloc(32)
        o_dttm = ar.alloc(32)
        o_AStm = ar.alloc(32)
        o_expAS = ar.alloc(32)
        o_ASf = ar.alloc(128)
        o_Ff = ar.alloc(128)
        o_wsb = ar.alloc(128)
        o_Wtm = ar.alloc(32)
        o_btm = ar.alloc(32)
        o_dcol = ar.alloc(1)
        o_drow = ar.alloc(32)
        o_dbsb = ar.alloc(H * P)
        o_g0 = ar.alloc(128)
        o_esb = ar.alloc(2 * 512)
        o_msb = ar.alloc(2 * 512)
        o_cfh = ar.alloc(2 * 64)
        o_cfsb = ar.alloc(4 * 128)
        o_xwh = ar.alloc(2 * 256)
        o_y0tm = ar.alloc(D)
        pan_i = [0]

        def panelB(dram_f, dram_h, dram_l, wk, first, w=1024):
            pi = pan_i[0] % NPB
            pan_i[0] += 1
            hi = ar.v(o_pph + pi * 512, (w + 1) // 2).bitcast(BF16)[:, :w]
            lo = ar.v(o_ppl + pi * 512, (w + 1) // 2).bitcast(BF16)[:, :w]
            if first:
                pv = ar.v(o_pfB, w)
                b.dma("sp", pv, dram_f, writes=["pf0"])
                b.op("act", lambda E, hi=hi, pv=pv: E.activation(hi, pv, AF.Copy),
                     reads=["pf0"], writes=[f"ph{pi}"])
                b.op("dve", lambda E, lo=lo, pv=pv, hi=hi:
                     E.tensor_tensor(lo, pv, hi, op=ALU.subtract),
                     reads=["pf0", f"ph{pi}"], writes=[f"pl{pi}"])
                b.dma("pool", dram_h, hi, reads=[f"ph{pi}"], writes=[wk], slot="wch")
                b.dma("pool", dram_l, lo, reads=[f"pl{pi}"], writes=[wk], slot="wcl")
            else:
                b.dma("sp", hi, dram_h, writes=[f"ph{pi}"], reads=[wk])
                b.dma("sp", lo, dram_l, writes=[f"pl{pi}"], reads=[wk])
            return hi, lo, pi

        PGRP = [(0, 8, 1024), (1, 8, 1024), (2, 8, 1024), (3, 8, 1024), (4, 1, 96)]

        for ti, (t0, tt) in enumerate(_tiles(T_ALL, TT_B)):
            ntc = tt // 128
            for c in range(ntc):
                kx = "xtm0"
                b.dma("pool", ar.v(o_xtm, D),
                      x1_s[t0 + c * 128:t0 + c * 128 + 128, :], writes=[kx])
                emit_ln_chunk(o_xtm, o_stat + 4 * c, o_sqt, kx, f"ln{c}")
                for f in range(DC):
                    pvw = psv(7, 0, 128)
                    b.op("pe", lambda E, f=f, pvw=pvw:
                         E.transpose(pvw, ar.v(o_sqt + f * 128, 128), ident),
                         reads=["lnsq", "c_ident"], writes=[BK[7]])
                    hi = ar.v(o_xhh + f * TT_B // 2, TT_B // 2).bitcast(BF16)[:, c * 128:c * 128 + 128]
                    lo = ar.v(o_xhl + f * TT_B // 2, TT_B // 2).bitcast(BF16)[:, c * 128:c * 128 + 128]
                    b.op("act", lambda E, hi=hi, pvw=pvw: E.activation(hi, pvw, AF.Copy),
                         reads=[BK[7]], writes=[f"xhh{f}"])
                    b.op("dve", lambda E, lo=lo, pvw=pvw, hi=hi:
                         E.tensor_tensor(lo, pvw, hi, op=ALU.subtract),
                         reads=[BK[7], f"xhh{f}"], writes=[f"xhl{f}"])
            for (mg, nmt, gcols) in PGRP:
                for k in range(DC):
                    whi, wlo, pi = panelB(
                        W_in[k * 128:(k + 1) * 128, mg * 1024:mg * 1024 + gcols],
                        W_in_h[k * 128:(k + 1) * 128, mg * 1024:mg * 1024 + gcols],
                        W_in_l[k * 128:(k + 1) * 128, mg * 1024:mg * 1024 + gcols],
                        "splitWin%d_%d" % (k, mg), ti == 0, gcols)
                    xhi = ar.v(o_xhh + k * TT_B // 2, TT_B // 2).bitcast(BF16)[:, 0:tt]
                    xlo = ar.v(o_xhl + k * TT_B // 2, TT_B // 2).bitcast(BF16)[:, 0:tt]
                    fns = []
                    for j in range(nmt):
                        w = min(128, gcols - j * 128)
                        fns += [
                            (lambda E, j=j, tt=tt, w=w, whi=whi, xhi=xhi, k=k:
                             E.matmul(psv(j, 0, tt, 0, w), whi[:, j * 128:j * 128 + w],
                                      xhi, start=(k == 0), stop=False)),
                            (lambda E, j=j, tt=tt, w=w, whi=whi, xlo=xlo:
                             E.matmul(psv(j, 0, tt, 0, w), whi[:, j * 128:j * 128 + w],
                                      xlo, start=False, stop=False)),
                            (lambda E, j=j, tt=tt, w=w, wlo=wlo, xhi=xhi, k=k:
                             E.matmul(psv(j, 0, tt, 0, w), wlo[:, j * 128:j * 128 + w],
                                      xhi, start=False, stop=(k == DC - 1))),
                        ]
                    b.mms(fns, reads=[f"ph{pi}", f"pl{pi}", f"xhh{k}", f"xhl{k}"],
                          claim=BK[:nmt] if k == 0 else (),
                          publish=BK[:nmt] if k == DC - 1 else ())
                for j in range(nmt):
                    pc = mg * 8 + j
                    bias = ar.t[:, o_bin + pc:o_bin + pc + 1]
                    if pc < DC:
                        gv = ar.v(o_gsb + (pc % 2) * TT_B, tt)
                        b.op("act", lambda E, gv=gv, j=j, bias=bias, tt=tt:
                             E.activation(gv, psv(j, 0, tt), AF.Silu, bias=bias),
                             reads=[BK[j], "c_bin"], writes=[f"gs{pc % 2}"])
                        own_lo = max(0, T_HALO - t0)
                        if own_lo < tt:
                            ot0 = t0 + own_lo - T_HALO
                            b.dma("pool", gs_s[pc, :, ot0:ot0 + (tt - own_lo)],
                                  ar.v(o_gsb + (pc % 2) * TT_B + own_lo, tt - own_lo),
                                  reads=[f"gs{pc % 2}"])
                    elif pc < 32:
                        cc = pc - DC
                        hv = ar.v(o_hcat + cc * (TT_B + 3) + 3, tt)
                        b.op("dve", lambda E, hv=hv, j=j, bias=bias, tt=tt:
                             E.tensor_scalar(hv, psv(j, 0, tt), bias, None, op0=ALU.add),
                             reads=[BK[j], "c_bin"], writes=[f"hc{cc}"])
                    else:
                        pv96 = ar.v(o_p32, tt, 0, 96)
                        b.op("dve", lambda E, pv96=pv96, j=j, tt=tt:
                             E.tensor_scalar(pv96, psv(j, 0, tt, 0, 96),
                                             ar.t[0:96, o_bin + 32:o_bin + 33], None, op0=ALU.add),
                             reads=[BK[j], "c_bin"], writes=["p32"])
            b.op("dve", lambda E, tt=tt:
                 E.tensor_copy(ar.v(o_hcat + 16 * (TT_B + 3) + 3, tt, 0, 64),
                               ar.v(o_p32, tt, 0, 64)),
                 reads=["p32"], writes=["hc16"])
            b.dma("pool", ar.v(o_dtraw, tt, 0, 32), ar.v(o_p32, tt, 64, 96),
                  reads=["p32"], writes=["dtraw"])
            if ti == 0:
                for cc in range(NCC):
                    nrow = 128 if cc < 16 else 64
                    hv = ar.v(o_hcat + cc * (TT_B + 3) + 3, 128, 0, nrow)
                    b.op("dve", lambda E, hv=hv, nrow=nrow:
                         E.tensor_scalar(hv, hv, ar.t[0:nrow, o_gates:o_gates + 1], None, op0=ALU.mult),
                         reads=[f"hc{cc}", "c_gates"], writes=[f"hc{cc}"])
            for cc in range(NCC):
                nrow = 128 if cc < 16 else 64
                b.op("dve", lambda E, cc=cc, nrow=nrow:
                     E.tensor_copy(ar.v(o_hcat + cc * (TT_B + 3), 3, 0, nrow),
                                   ar.v(o_tail + cc * 3, 3, 0, nrow)),
                     reads=["tail"], writes=[f"hc{cc}"])
            for cc in range(NCC):
                nrow = 128 if cc < 16 else 64
                a1 = ar.v(o_ca1, tt, 0, nrow)
                a2v = ar.v(o_ca2, tt, 0, nrow)
                hcs = [ar.v(o_hcat + cc * (TT_B + 3) + k, tt, 0, nrow) for k in range(4)]
                ws = [ar.t[0:nrow, o_convw + cc * 4 + k:o_convw + cc * 4 + k + 1] for k in range(4)]
                b.op("dve", lambda E, a1=a1, hc0=hcs[0], w0=ws[0]:
                     E.tensor_scalar(a1, hc0, w0, None, op0=ALU.mult),
                     reads=[f"hc{cc}", "c_convw"], writes=["cacc1"])
                b.op("dve", lambda E, a2v=a2v, hc1=hcs[1], w1=ws[1], a1=a1:
                     E.scalar_tensor_tensor(a2v, hc1, w1, a1, op0=ALU.mult, op1=ALU.add),
                     reads=[f"hc{cc}", "c_convw", "cacc1"], writes=["cacc2"])
                b.op("dve", lambda E, a1=a1, hc2=hcs[2], w2=ws[2], a2v=a2v:
                     E.scalar_tensor_tensor(a1, hc2, w2, a2v, op0=ALU.mult, op1=ALU.add),
                     reads=[f"hc{cc}", "c_convw", "cacc2"], writes=["cacc1"])
                b.op("dve", lambda E, a2v=a2v, hc3=hcs[3], w3=ws[3], a1=a1:
                     E.scalar_tensor_tensor(a2v, hc3, w3, a1, op0=ALU.mult, op1=ALU.add),
                     reads=[f"hc{cc}", "c_convw", "cacc1"], writes=["cacc2"])
                xsv = ar.v(o_xs + cc * TT_B, tt, 0, nrow)
                cb = ar.t[0:nrow, o_convb + cc:o_convb + cc + 1]
                b.op("act", lambda E, xsv=xsv, a2v=a2v, cb=cb:
                     E.activation(xsv, a2v, AF.Silu, bias=cb),
                     reads=["cacc2", "c_convb"], writes=[f"xs{cc}"])
            b.op("dve", lambda E, tt=tt: E.tensor_copy(
                ar.v(o_tail, NCC * 3).rearrange("p (c k) -> p c k", k=3),
                ar.t[:, o_hcat:o_hcat + NCC * (TT_B + 3)]
                    .rearrange("p (c q) -> p c q", q=TT_B + 3)[:, :, tt:tt + 3]),
                 reads=[f"hc{cc}" for cc in range(NCC)], writes=["tail"])
            b.dma("pool", ar.v(o_bf0, tt, 0, 32), ar.v(o_xs + 16 * TT_B, tt, 0, 32),
                  reads=["xs16"], writes=["bf0"])
            b.dma("pool", ar.v(o_cf0, tt, 0, 32), ar.v(o_xs + 16 * TT_B, tt, 32, 64),
                  reads=["xs16"], writes=["cf0"])
            dtb = ar.v(o_dtb, 1, 0, 32)
            b.op("act", lambda E, tt=tt: E.activation(ar.v(o_dts1, tt, 0, 32),
                                                      ar.v(o_dtraw, tt, 0, 32), AF.Abs, bias=dtb),
                 reads=["dtraw", "c_dtb"], writes=["dts1"])
            b.op("act", lambda E, tt=tt: E.activation(ar.v(o_dts1, tt, 0, 32),
                                                      ar.v(o_dts1, tt, 0, 32), AF.Exp, scale=-1.0),
                 reads=["dts1"], writes=["dts1"])
            b.op("act", lambda E, tt=tt: E.activation(ar.v(o_dts1, tt, 0, 32),
                                                      ar.v(o_dts1, tt, 0, 32), AF.Ln, bias=1.0),
                 reads=["dts1"], writes=["dts1"])
            b.op("act", lambda E, tt=tt: E.activation(ar.v(o_dts2, tt, 0, 32),
                                                      ar.v(o_dtraw, tt, 0, 32), AF.Relu, bias=dtb),
                 reads=["dtraw", "c_dtb"], writes=["dts2"])
            b.op("dve", lambda E, tt=tt: E.tensor_add(ar.v(o_dtf, tt, 0, 32),
                                                      ar.v(o_dts1, tt, 0, 32),
                                                      ar.v(o_dts2, tt, 0, 32)),
                 reads=["dts1", "dts2"], writes=["dtf"])
            b.op("dve", lambda E, tt=tt: E.tensor_scalar(ar.v(o_dtA, tt, 0, 32),
                                                         ar.v(o_dtf, tt, 0, 32),
                                                         ar.v(o_alog, 1, 0, 32), None, op0=ALU.mult),
                 reads=["dtf", "c_alog"], writes=["dtA"])

            for cj in range(ntc):
                cg = (t0 // 128) + cj - 1
                if cg < 0:
                    continue
                cs = cj * 128
                emit_tp(ar.v(o_dtA + cs, 128, 0, 32), ar.v(o_dtAtm, 32),
                        "dtA", "dtAtm", 7, 0, 32, 128)
                emit_tp(ar.v(o_dtf + cs, 128, 0, 32), ar.v(o_dttm, 32),
                        "dtf", "dttm", 7, 128, 32, 128)
                b.mms([lambda E: E.matmul(psv(7, 256, 32), tri, ar.v(o_dtAtm, 32),
                                          start=True, stop=True)],
                      reads=["c_tri", "dtAtm"], writes=[BK[7]])
                b.op("dve", lambda E: E.tensor_copy(ar.v(o_AStm, 32), psv(7, 256, 32)),
                     reads=[BK[7]], writes=["AStm"])
                emit_tp(ar.v(o_AStm, 32), ar.v(o_ASf, 128, 0, 32),
                        "AStm", "ASf", 7, 288, 128, 32)
                b.op("act", lambda E: E.activation(ar.v(o_expAS, 32), ar.v(o_AStm, 32), AF.Exp),
                     reads=["AStm"], writes=["expAS"])
                b.op("act", lambda E: E.activation(ar.v(o_Ff, 128, 0, 32),
                                                   ar.v(o_ASf, 128, 0, 32), AF.Exp,
                                                   bias=ar.v(o_lc, 1, 0, 32)),
                     reads=["ASf", "lc"], writes=["Ff"])
                emit_tp(ar.v(o_Ff, 128, 0, 32), ar.v(o_ftm + cg * 32, 32),
                        "Ff", "ftm", 7, 0, 32, 128)
                b.op("dve", lambda E: E.tensor_scalar(ar.v(o_wsb, 128, 0, 32),
                                                      ar.v(o_ASf, 128, 0, 32),
                                                      ar.t[0:32, o_ASf + 127:o_ASf + 128],
                                                      None, op0=ALU.subtract),
                     reads=["ASf"], writes=["wsb"])
                b.op("act", lambda E: E.activation(ar.v(o_wsb, 128, 0, 32),
                                                   ar.v(o_wsb, 128, 0, 32), AF.Exp, scale=-1.0),
                     reads=["wsb"], writes=["wsb"])
                b.op("dve", lambda E, cs=cs: E.tensor_mul(ar.v(o_wsb, 128, 0, 32),
                                                          ar.v(o_wsb, 128, 0, 32),
                                                          ar.v(o_dtf + cs, 128, 0, 32)),
                     reads=["wsb", "dtf"], writes=["wsb"])
                emit_tp(ar.v(o_wsb, 128, 0, 32), ar.v(o_Wtm, 32),
                        "wsb", "Wtm", 7, 128, 32, 128)
                b.op("act", lambda E: E.activation(ar.v(o_dcol, 1, 0, 32),
                                                   ar.t[0:32, o_ASf + 127:o_ASf + 128], AF.Exp),
                     reads=["ASf"], writes=["dcol"])
                b.op("dve", lambda E: E.tensor_add(ar.v(o_lc, 1, 0, 32), ar.v(o_lc, 1, 0, 32),
                                                   ar.t[0:32, o_ASf + 127:o_ASf + 128]),
                     reads=["ASf", "Ff", "lc"], writes=["lc"])
                emit_tp(ar.v(o_dcol, 1, 0, 32), ar.t[0:1, o_drow:o_drow + 32],
                        "dcol", "drow", 7, 256, 32, 1)
                b.mms([lambda E: E.matmul(psv(6, 0, 32, 0, 32), ones_lhsT(32),
                                          ar.t[0:1, o_drow:o_drow + 32],
                                          start=True, stop=True)],
                      reads=["drow", "c_ones"], writes=[BK[6]])
                b.op("dve", lambda E: E.tensor_copy(ar.v(o_dbsb, 32, 0, 32),
                                                    psv(6, 0, 32, 0, 32)),
                     reads=[BK[6]], writes=["dbsb0"])
                b.mms([lambda E, cs=cs: E.matmul(psv(6, 0, 128),
                                                 ar.v(o_bf0 + cs, 128, 0, 32),
                                                 ar.v(o_cf0 + cs, 128, 0, 32),
                                                 start=True, stop=True)],
                      reads=["bf0", "cf0"], writes=[BK[6]])
                b.op("dve", lambda E: E.tensor_copy(ar.v(o_g0, 128), psv(6, 0, 128)),
                     reads=[BK[6]], writes=["g0"])
                emit_tp(ar.v(o_cf0 + cs, 128, 0, 32), ar.v(o_ctm + cg * 32, 32),
                        "cf0", "ctm", 7, 0, 32, 128)
                emit_tp(ar.v(o_bf0 + cs, 128, 0, 32), ar.v(o_btm, 32),
                        "bf0", "btm", 7, 128, 32, 128)
                for f in range(DC):
                    emit_tp(ar.v(o_xs + f * TT_B + cs, 128), ar.v(o_xstm + f * 128, 128),
                            f"xs{f}", f"xstm{f}", 7, 256, 128, 128)
                if debug and cg == 0:
                    b.dma("pool", d_AS[:, :], ar.v(o_AStm, 32), reads=["AStm"])
                    b.dma("pool", d_dtf[:, :], ar.v(o_dtf, TT_B, 0, 32), reads=["dtf"])
                    b.dma("pool", d_xs0[:, :], ar.v(o_xs + 0 * TT_B + cs, 128), reads=["xs0"])
                    b.dma("pool", d_b[:, :], ar.v(o_bf0, TT_B, 0, 32), reads=["bf0"])
                    b.dma("pool", d_c[:, :], ar.v(o_cf0, TT_B, 0, 32), reads=["cf0"])
                    b.dma("pool", d_g0[:, :], ar.v(o_g0, 128), reads=["g0"])
                    b.dma("pool", d_wtm[:, :], ar.v(o_Wtm, 32), reads=["Wtm"])
                    b.dma("pool", d_db[:, :], ar.v(o_dbsb, H * P, 0, 32),
                          reads=["dbsb0", "dbsb1", "dbsb2", "dbsb3"])
                for g4 in range(H // 4):
                    h0 = 4 * g4
                    gp = g4 % 2
                    fns = [
                        (lambda E, q=q, h=h0 + q:
                         E.matmul(psv(6, q * 128, 128),
                                  ar.t[0:32, o_ident + h:o_ident + h + 1].broadcast_to([32, 128]),
                                  ar.v(o_ASf, 128, 0, 32), start=True, stop=True))
                        for q in range(4)
                    ]
                    b.mms(fns, reads=["ASf", "c_ident"], writes=[BK[6]])
                    esb4 = ar.v(o_esb + gp * 512, 512)
                    msb4 = ar.v(o_msb + gp * 512, 512)
                    ke, km = f"esb{gp}", f"msb{gp}"
                    b.op("dve", lambda E, esb4=esb4, h0=h0:
                         E.tensor_tensor(esb4, psv(6, 0, 512),
                                         ar.t[:, o_AStm + h0:o_AStm + h0 + 4]
                                         .rearrange("p (h o) -> p h o", o=1)
                                         .broadcast_to([128, 4, 128]), op=ALU.subtract),
                         reads=[BK[6], "AStm"], writes=[ke])
                    b.op("dve", lambda E, esb4=esb4:
                         E.tensor_tensor(esb4, esb4,
                                         ar.v(o_mask, 128)
                                         .rearrange("p (a t) -> p a t", a=1)
                                         .broadcast_to([128, 4, 128]), op=ALU.add),
                         reads=[ke, "c_mask"], writes=[ke])
                    b.op("act", lambda E, msb4=msb4, esb4=esb4:
                         E.activation(msb4, esb4, AF.Exp),
                         reads=[ke], writes=[km])
                    b.op("dve", lambda E, msb4=msb4, h0=h0:
                         E.tensor_tensor(msb4, msb4,
                                         ar.t[:, o_dttm + h0:o_dttm + h0 + 4]
                                         .rearrange("p (h o) -> p h o", o=1)
                                         .broadcast_to([128, 4, 128]), op=ALU.mult),
                         reads=[km, "dttm"], writes=[km])
                    b.op("dve", lambda E, msb4=msb4:
                         E.tensor_tensor(msb4, msb4,
                                         ar.v(o_g0, 128)
                                         .rearrange("p (a t) -> p a t", a=1)
                                         .broadcast_to([128, 4, 128]), op=ALU.mult),
                         reads=[km, "g0"], writes=[km])
                    for q in range(4):
                        h = h0 + q
                        a8 = h % 8
                        yb = 4 + ((h // 8) % 2)
                        b.mms([(lambda E, h=h, yb=yb, a8=a8, msb4=msb4, q=q:
                                E.matmul(psv(yb, a8 * 64, 64), msb4[:, q * 128:(q + 1) * 128],
                                         ar.t[:, o_xstm + h * 64:o_xstm + (h + 1) * 64],
                                         start=(a8 == 0), stop=(a8 == 7)))],
                              reads=[f"xstm{h // 2}", km],
                              claim=[BK[yb]] if a8 == 0 else (),
                              publish=[BK[yb]] if a8 == 7 else ())
                        b.mms([(lambda E, h=h, a8=a8, cs=cs:
                                E.matmul(psv(7, a8 * 64, 64),
                                         ar.v(o_cf0 + cs, 128, 0, 32),
                                         ar.t[0:32, o_state + h * 64:o_state + (h + 1) * 64],
                                         start=(a8 == 0), stop=(a8 == 7)))],
                              reads=["cf0", "state"],
                              claim=[BK[7]] if a8 == 0 else (),
                              publish=[BK[7]] if a8 == 7 else ())
                        if a8 == 7:
                            g8 = h // 8
                            b.op("dve", lambda E, g8=g8, yb=yb:
                                 E.tensor_copy(ar.v(o_y0tm + g8 * 512, 512), psv(yb, 0, 512)),
                                 reads=[BK[yb]], writes=["y0tm"])
                            for qq in range(8):
                                hh = g8 * 8 + qq
                                b.op("dve", lambda E, g8=g8, qq=qq, hh=hh:
                                     E.scalar_tensor_tensor(
                                         ar.v(o_y0tm + g8 * 512 + qq * 64, 64),
                                         psv(7, qq * 64, 64),
                                         ar.t[:, o_expAS + hh:o_expAS + hh + 1],
                                         ar.v(o_y0tm + g8 * 512 + qq * 64, 64),
                                         op0=ALU.mult, op1=ALU.add),
                                     reads=[BK[7], "expAS", "y0tm"], writes=["y0tm"])
                    xw4 = ar.v(o_xwh + gp * 256, 256)
                    kxw = f"xw{gp}"
                    b.op("dve", lambda E, xw4=xw4, h0=h0:
                         E.tensor_tensor(xw4, ar.t[:, o_xstm + h0 * 64:o_xstm + (h0 + 4) * 64],
                                         ar.t[:, o_Wtm + h0:o_Wtm + h0 + 4]
                                         .rearrange("p (h o) -> p h o", o=1)
                                         .broadcast_to([128, 4, 64]), op=ALU.mult),
                         reads=[f"xstm{h0 // 2}", f"xstm{h0 // 2 + 1}", "Wtm"], writes=[kxw])
                    b.mms([lambda E, h0=h0, xw4=xw4:
                           E.matmul(ps_big[0:32, h0 * 64:(h0 + 4) * 64], ar.v(o_btm, 32),
                                    xw4, start=(h0 % 8 == 0), stop=(h0 % 8 == 4))],
                          reads=["btm", kxw],
                          claim=[BK[h0 // 8]] if h0 % 8 == 0 else (),
                          publish=[BK[h0 // 8]] if h0 % 8 == 4 else ())
                for g8 in range(4):
                    stv = ar.v(o_state + g8 * 512, 512, 0, 32)
                    b.op("dve", lambda E, g8=g8, stv=stv:
                         E.tensor_tensor(stv, stv,
                                         ar.t[0:32, o_dbsb + g8 * 8:o_dbsb + (g8 + 1) * 8]
                                         .rearrange("p (h o) -> p h o", o=1)
                                         .broadcast_to([32, 8, 64]), op=ALU.mult),
                         reads=["state", "dbsb0"], writes=["state"])
                    b.op("dve", lambda E, g8=g8, stv=stv:
                         E.tensor_add(stv, stv, ps_big[0:32, g8 * 512:(g8 + 1) * 512]),
                         reads=["state"] + [BK[g8]], writes=["state"])
                if debug and cg == 0:
                    b.dma("pool", d_state[:, :], ar.v(o_state, H * P, 0, 32), reads=["state"])
                b.op("dve", lambda E: E.tensor_mul(ar.v(o_sqt, D), ar.v(o_xstm, D),
                                                   ar.v(o_drepb, D)),
                     reads=["drepb"] + [f"xstm{f}" for f in range(DC)], writes=["lnsq"])
                b.op("dve", lambda E: E.tensor_add(ar.v(o_y0tm, D), ar.v(o_y0tm, D),
                                                   ar.v(o_sqt, D)),
                     reads=["lnsq", "y0tm"], writes=["y0tm"])
                b.dma("pool", y0_s[cg * 128:(cg + 1) * 128, :], ar.v(o_y0tm, D),
                      reads=["y0tm"])
        ar.pop()
        b.barrier_all()

    if "CC" in stages:
        ar.push()
        o_ccs = ar.alloc(H * P)
        if debug:
            b.dma("pool", d_stfin[:, :], ar.v(o_state, H * P, 0, 32), reads=["state"])
        b.op("dve", lambda E: E.tensor_scalar(ar.v(o_ccs, H * P, 0, 32),
                                              ar.v(o_state, H * P, 0, 32),
                                              ar.t[0:32, o_gates + 1:o_gates + 2],
                                              None, op0=ALU.mult),
             reads=["state", "c_gates"], writes=["ccs"])
        b.dma("pool", cc_in[:, :], ar.v(o_ccs, H * P, 0, 32), reads=["ccs"], writes=["ccin"])
        cc_sem = b.cc_sem
        ccin_sem = b.dsem["w:ccin"]
        ccin_cnt = b.dcnt["w:ccin"]

        def _cc(E):
            E.wait_ge(ccin_sem, ccin_cnt)
            E.collective_compute(
                "AllReduce", ALU.add,
                replica_groups=[[0, 1], [2, 3], [4, 5], [6, 7]],
                ins=[cc_in[:, :].opt()], outs=[cc_out[:, :].opt()],
            ).then_inc(cc_sem, 1)
            E.wait_ge(cc_sem, 1)
        b.prog["pool"].append(_cc)
        b.icount["pool"] += 3
        b.dma("pool", ar.v(o_hinit, H * P, 0, 32), cc_out[:, :], writes=["hinit"])
        b.op("dve", lambda E: E.tensor_scalar(ar.v(o_hinit, H * P, 0, 32),
                                              ar.v(o_hinit, H * P, 0, 32),
                                              ar.t[0:32, o_gates:o_gates + 1],
                                              None, op0=ALU.mult),
             reads=["hinit", "c_gates"], writes=["hinit"])
        if debug:
            b.dma("pool", hin_dbg[:, :], ar.v(o_hinit, H * P, 0, 32), reads=["hinit"])
        ar.pop()
        b.barrier_all()

    if "C1" in stages:
        ar.push()
        o_ysbc = ar.alloc(DC * TT_C)
        o_gsbc = ar.alloc(DC * TT_C)
        o_yhc = ar.alloc(DC * TT_C // 2)
        o_ylc = ar.alloc(DC * TT_C // 2)
        o_y0tmc = ar.alloc(2 * D)
        o_x1tmc = ar.alloc(2 * D)
        NPC = 6
        o_pch = ar.alloc(NPC * 512)
        o_pcl = ar.alloc(NPC * 512)
        o_pfC = ar.alloc(1024)
        o_sqtc = ar.alloc(TT_C)
        o_rrowc = ar.alloc(TT_C)
        o_rbc = ar.alloc(TT_C)
        o_cf2c = ar.alloc(32)
        o_cf2sbc = ar.alloc(128)
        o_wtmpc = ar.alloc(TT_C)
        pan_ic = [0]

        def panelC(dram_f, dram_h, dram_l, wk, first, w=1024):
            pi = pan_ic[0] % NPC
            pan_ic[0] += 1
            hi = ar.v(o_pch + pi * 512, (w + 1) // 2).bitcast(BF16)[:, :w]
            lo = ar.v(o_pcl + pi * 512, (w + 1) // 2).bitcast(BF16)[:, :w]
            if first:
                pv = ar.v(o_pfC, w)
                b.dma("sp", pv, dram_f, writes=["pf0"])
                b.op("act", lambda E, hi=hi, pv=pv: E.activation(hi, pv, AF.Copy),
                     reads=["pf0"], writes=[f"ph{pi}"])
                b.op("dve", lambda E, lo=lo, pv=pv, hi=hi:
                     E.tensor_tensor(lo, pv, hi, op=ALU.subtract),
                     reads=["pf0", f"ph{pi}"], writes=[f"pl{pi}"])
                b.dma("pool", dram_h, hi, reads=[f"ph{pi}"], writes=[wk], slot="wch")
                b.dma("pool", dram_l, lo, reads=[f"pl{pi}"], writes=[wk], slot="wcl")
            else:
                b.dma("sp", hi, dram_h, writes=[f"ph{pi}"], reads=[wk])
                b.dma("sp", lo, dram_l, writes=[f"pl{pi}"], reads=[wk])
            return hi, lo, pi

        for ti, (t0, tt) in enumerate(_tiles(T_OWN, TT_C)):
            ntc = tt // 128
            for f in range(DC):
                b.dma("pool", ar.v(o_gsbc + f * TT_C, tt), gs_s[f, :, t0:t0 + tt],
                      writes=[f"gsb{f}"])
            for c in range(ntc):
                b.dma("pool", ar.v(o_y0tmc + c * D, D),
                      y0_s[t0 + c * 128:t0 + c * 128 + 128, :], writes=[f"y0tm{c}"])
                b.dma("pool", ar.v(o_x1tmc + c * D, D),
                      x1_s[T_HALO + t0 + c * 128:T_HALO + t0 + c * 128 + 128, :],
                      writes=[f"x1tm{c}"])
            for cj in range(ntc):
                cg = t0 // 128 + cj
                for h in range(H):
                    a8 = h % 8
                    yb = 4 + ((h // 8) % 2)
                    b.op("dve", lambda E, h=h, cg=cg:
                         E.tensor_scalar(ar.v(o_cf2c, 32), ar.v(o_ctm + cg * 32, 32),
                                         ar.t[:, o_ftm + cg * 32 + h:o_ftm + cg * 32 + h + 1],
                                         None, op0=ALU.mult),
                         reads=["ctm", "ftm"], writes=["cf2"])
                    emit_tp(ar.v(o_cf2c, 32), ar.v(o_cf2sbc, 128, 0, 32),
                            "cf2", "cf2sb", 7, 0, 128, 32)
                    b.mms([lambda E, h=h, yb=yb, a8=a8:
                           E.matmul(psv(yb, a8 * 64, 64), ar.v(o_cf2sbc, 128, 0, 32),
                                    ar.t[0:32, o_hinit + h * 64:o_hinit + (h + 1) * 64],
                                    start=(a8 == 0), stop=(a8 == 7))],
                          reads=["hinit", "cf2sb"],
                          claim=[BK[yb]] if a8 == 0 else (),
                          publish=[BK[yb]] if a8 == 7 else ())
                    if a8 == 7:
                        g8 = h // 8
                        yv = ar.v(o_y0tmc + cj * D + g8 * 512, 512)
                        b.op("dve", lambda E, yv=yv, yb=yb:
                             E.tensor_add(yv, yv, psv(yb, 0, 512)),
                             reads=[BK[yb], f"y0tm{cj}"], writes=[f"y0tm{cj}"])
                for f in range(DC):
                    emit_tp(ar.v(o_y0tmc + cj * D + f * 128, 128),
                            ar.v(o_ysbc + f * TT_C + cj * 128, 128),
                            f"y0tm{cj}", f"ysb{f}", 7, 0, 128, 128)
                for f in range(DC):
                    yv = ar.v(o_ysbc + f * TT_C + cj * 128, 128)
                    gv = ar.v(o_gsbc + f * TT_C + cj * 128, 128)
                    b.op("dve", lambda E, yv=yv, gv=gv:
                         E.tensor_mul(yv, yv, gv),
                         reads=[f"gsb{f}", f"ysb{f}"], writes=[f"ysb{f}"])
            for f in range(DC):
                sq = ar.v(o_sqtc, tt)
                b.op("act", lambda E, f=f, sq=sq, tt=tt:
                     E.activation(sq, ar.v(o_ysbc + f * TT_C, tt), AF.Square),
                     reads=[f"ysb{f}"], writes=["sqt"])
                b.mms([lambda E, sq=sq, f=f, tt=tt:
                       E.matmul(psv(6, 0, tt, 0, 1), ones_col(), sq,
                                start=(f == 0), stop=(f == DC - 1))],
                      reads=["sqt", "c_ones"],
                      claim=[BK[6]] if f == 0 else (),
                      publish=[BK[6]] if f == DC - 1 else ())
            b.op("act", lambda E, tt=tt: E.activation(ar.t[0:1, o_rrowc:o_rrowc + tt],
                                                      psv(6, 0, tt, 0, 1), AF.Sqrt,
                                                      scale=1.0 / D, bias=EPS_RMS),
                 reads=[BK[6], "c_eps"], writes=["rrow"])
            b.op("dve", lambda E, tt=tt: E.reciprocal(ar.t[0:1, o_rrowc:o_rrowc + tt],
                                                      ar.t[0:1, o_rrowc:o_rrowc + tt]),
                 reads=["rrow"], writes=["rrow"])
            b.mms([lambda E, tt=tt: E.matmul(psv(7, 0, tt), ones_lhsT(128),
                                             ar.t[0:1, o_rrowc:o_rrowc + tt], start=True, stop=True)],
                  reads=["rrow", "c_ones"], writes=[BK[7]])
            b.op("dve", lambda E, tt=tt: E.tensor_copy(ar.v(o_rbc, tt), psv(7, 0, tt)),
                 reads=[BK[7]], writes=["rb"])
            for f in range(DC):
                yv = ar.v(o_ysbc + f * TT_C, tt)
                b.op("dve", lambda E, yv=yv, f=f, tt=tt:
                     E.scalar_tensor_tensor(yv, yv, ar.t[:, o_normw + f:o_normw + f + 1],
                                            ar.v(o_rbc, tt), op0=ALU.mult, op1=ALU.mult),
                     reads=["rb", "c_normw", f"ysb{f}"], writes=[f"ysb{f}"])
                hiv = ar.v(o_yhc + f * TT_C // 2, TT_C // 2).bitcast(BF16)[:, :tt]
                lov = ar.v(o_ylc + f * TT_C // 2, TT_C // 2).bitcast(BF16)[:, :tt]
                b.op("act", lambda E, hiv=hiv, yv=yv: E.activation(hiv, yv, AF.Copy),
                     reads=[f"ysb{f}"], writes=[f"yh{f}"])
                b.op("dve", lambda E, lov=lov, yv=yv, hiv=hiv:
                     E.tensor_tensor(lov, yv, hiv, op=ALU.subtract),
                     reads=[f"ysb{f}", f"yh{f}"], writes=[f"yl{f}"])
            for mg in range(2):
                for k in range(DC):
                    whi, wlo, pi = panelC(
                        W_out[k * 128:(k + 1) * 128, mg * 1024:(mg + 1) * 1024],
                        W_out_h[k * 128:(k + 1) * 128, mg * 1024:(mg + 1) * 1024],
                        W_out_l[k * 128:(k + 1) * 128, mg * 1024:(mg + 1) * 1024],
                        "splitWout%d_%d" % (k, mg), ti == 0)
                    yhiv = ar.v(o_yhc + k * TT_C // 2, TT_C // 2).bitcast(BF16)[:, :tt]
                    ylov = ar.v(o_ylc + k * TT_C // 2, TT_C // 2).bitcast(BF16)[:, :tt]
                    fns = []
                    for j in range(8):
                        fns += [
                            (lambda E, j=j, tt=tt, whi=whi, yhiv=yhiv, k=k:
                             E.matmul(psv(j, 0, tt), whi[:, j * 128:(j + 1) * 128], yhiv,
                                      start=(k == 0), stop=False)),
                            (lambda E, j=j, tt=tt, whi=whi, ylov=ylov:
                             E.matmul(psv(j, 0, tt), whi[:, j * 128:(j + 1) * 128], ylov,
                                      start=False, stop=False)),
                            (lambda E, j=j, tt=tt, wlo=wlo, yhiv=yhiv, k=k:
                             E.matmul(psv(j, 0, tt), wlo[:, j * 128:(j + 1) * 128], yhiv,
                                      start=False, stop=(k == DC - 1))),
                        ]
                    b.mms(fns, reads=[f"ph{pi}", f"pl{pi}", f"yh{k}", f"yl{k}"],
                          claim=BK if k == 0 else (),
                          publish=BK if k == DC - 1 else ())
                for j in range(8):
                    f = mg * 8 + j
                    wv = ar.v(o_wtmpc, tt)
                    b.op("act", lambda E, wv=wv, j=j, tt=tt:
                         E.activation(wv, psv(j, 0, tt), AF.Copy),
                         reads=[BK[j]], writes=["wtmp"])
                    for c in range(ntc):
                        b.op("pe", lambda E, c=c, j=j:
                             E.transpose(psv(j, 0, 128), ar.v(o_wtmpc + c * 128, 128), ident),
                             reads=["wtmp", "c_ident"], writes=[BK[j]])
                        xsl = ar.v(o_x1tmc + c * D + f * 128, 128)
                        b.op("dve", lambda E, xsl=xsl, j=j:
                             E.tensor_add(xsl, xsl, psv(j, 0, 128)),
                             reads=[BK[j], f"x1tm{c}"], writes=[f"x1tm{c}"])
            for c in range(ntc):
                b.dma("pool", x2_s[t0 + c * 128:t0 + c * 128 + 128, :],
                      ar.v(o_x1tmc + c * D, D), reads=[f"x1tm{c}"])
        ar.pop()
        b.barrier_all()

    if "C2" in stages:
        emit_ffn_stage(T_OWN,
                       lambda t, n: x2_s[t:t + n, :],
                       lambda t, n: out_t[t:t + n, :], False)
        b.barrier_all()

    with nc.Block() as block:
        b.finalize(block)
    return nc, es


def _pad_cols(v, n):
    out = np.zeros(n, np.float32)
    out[:v.shape[0]] = v
    return out


def make_in_maps(inputs):
    x = np.ascontiguousarray(np.asarray(inputs["L0_attn_x"], np.float32))
    W_in = np.ascontiguousarray(np.asarray(inputs["L0_attn_W_in"], np.float32))
    b_in = np.asarray(inputs["L0_attn_b_in"], np.float32)
    conv_w = np.asarray(inputs["L0_attn_conv_w"], np.float32)
    conv_b = np.asarray(inputs["L0_attn_conv_b"], np.float32)
    A_log = np.asarray(inputs["L0_attn_A_log"], np.float32)
    dt_bias = np.asarray(inputs["L0_attn_dt_bias"], np.float32)
    D_param = np.asarray(inputs["L0_attn_D_param"], np.float32)
    norm_w = np.asarray(inputs["L0_attn_norm_w"], np.float32)
    W_out = np.ascontiguousarray(np.asarray(inputs["L0_attn_W_out"], np.float32))
    wgu = np.asarray(inputs["L0_ffn_w_gate_up"], np.float32)
    wd = np.asarray(inputs["L0_ffn_w_down"], np.float32)

    WguT = np.ascontiguousarray(wgu.T)
    WdT = np.ascontiguousarray(wd.T)

    b_in_t = np.ascontiguousarray(_pad_cols(b_in, 33 * 128).reshape(33, 128).T)
    cwp = np.zeros((NCC * 128, 4), np.float32)
    cwp[:CONV] = conv_w
    conv_w_t = np.ascontiguousarray(
        cwp.reshape(NCC, 128, 4).transpose(1, 0, 2).reshape(128, NCC * 4))
    conv_b_t = np.ascontiguousarray(_pad_cols(conv_b, NCC * 128).reshape(NCC, 128).T)
    d_rep_t = np.ascontiguousarray(np.repeat(D_param, P).reshape(DC, 128).T)
    norm_w_t = np.ascontiguousarray(norm_w.reshape(DC, 128).T)

    cons = np.zeros((128, 3 * 128 + 3), np.float32)
    cons[:, 0:128] = np.eye(128, dtype=np.float32)
    cons[:, 128:256] = np.triu(np.ones((128, 128), np.float32))
    cons[:, 256:384] = np.where(np.triu(np.ones((128, 128), np.float32)) > 0, 0.0, MASK_NEG)
    cons[:, 384] = 1.0
    cons[:, 385] = EPS_LN
    cons[:, 386] = EPS_RMS

    B, L, _ = x.shape
    Th = L // 2
    in_maps = []
    for core in range(8):
        bb, half = core // 2, core % 2
        own = x[bb, half * Th:(half + 1) * Th]
        halo = x[bb, Th - T_HALO:Th] if half == 1 else np.zeros((T_HALO, D), np.float32)
        x_pre_a = np.ascontiguousarray(np.concatenate([halo, own], 0))
        g = np.zeros((128, 2), np.float32)
        g[:, 0] = 1.0 if half == 1 else 0.0
        g[:, 1] = 1.0 if half == 0 else 0.0
        in_maps.append({
            "x_pre": x_pre_a, "WguT": WguT, "WdT": WdT, "W_in": W_in, "W_out": W_out,
            "b_in_t": b_in_t, "conv_w_t": conv_w_t, "conv_b_t": conv_b_t,
            "d_rep_t": d_rep_t, "d_rep_row": np.ascontiguousarray(np.repeat(D_param, P).reshape(1, D)),
            "norm_w_t": norm_w_t,
            "a_log": np.ascontiguousarray(A_log.reshape(32, 1)),
            "dt_bias_t": np.ascontiguousarray(dt_bias.reshape(32, 1)),
            "consts": cons, "gates_t": g,
        })
    return in_maps


def kernel(**inputs):
    in_maps = make_in_maps(inputs)
    nc, es = build_program(debug=False)
    try:
        res = run_bass_kernel_spmd(nc, in_maps, core_ids=list(range(8)))
    finally:
        es.close()
    x = np.asarray(inputs["L0_attn_x"])
    B, L, _ = x.shape
    Th = L // 2
    out = np.empty((B, L, D), np.float32)
    for core in range(8):
        bb, half = core // 2, core % 2
        out[bb, half * Th:(half + 1) * Th] = res.results[core]["out"]
    return out


# version 16
# speedup vs baseline: 1.1005x; 1.1005x over previous
import numpy as np
from contextlib import ExitStack
import concourse.bass as bass
import concourse.mybir as mybir
from concourse.bass_utils import run_bass_kernel_spmd

F32 = mybir.dt.float32
BF16 = mybir.dt.bfloat16
AF = mybir.ActivationFunctionType
ALU = mybir.AluOpType
AXX = mybir.AxisListType.X

H, P, N, KC = 32, 64, 32, 4
D = 2048
DC = 16
CONV = 2112
NCC = 17
PROJ = 4192
INTER = 8192
IC = 64
T_OWN = 2048
T_HALO = 128
T_ALL = 2176
NCH = 16
EPS_LN = 1e-5
EPS_RMS = 1e-6
MASK_NEG = -1e30

TT_A = 384
TT_B = 384
TT_C = 256
NPAN = 4


def _tiles(total, tt):
    out, t0 = [], 0
    while t0 < total:
        out.append((t0, min(tt, total - t0)))
        t0 += out[-1][1]
    return out


class Builder:
    ENGINES = ("pe", "dve", "act", "sp", "pool")

    def __init__(self, nc):
        self.nc = nc
        self.prog = {e: [] for e in self.ENGINES}
        self.cnt = {e: 0 for e in self.ENGINES}
        self.implied = {e: {} for e in self.ENGINES}
        self.drain_cnt = {e: 0 for e in self.ENGINES}
        self.icount = {e: 0 for e in self.ENGINES}
        self.bufs = {}
        self.sems = {}
        self.dsem = {}
        self.dcnt = {}
        self.cc_sem = None
        self.es = None

    def make_sems(self, es):
        self.es = es
        for e in self.ENGINES:
            self.sems[e] = es.enter_context(self.nc.semaphore(f"s_{e}"))
        self.cc_sem = es.enter_context(self.nc.semaphore("s_cc"))

    def _slot_sem(self, slot):
        if slot not in self.dsem:
            self.dsem[slot] = self.es.enter_context(
                self.nc.semaphore(f"d_{len(self.dsem)}"))
            self.dcnt[slot] = 0
        return self.dsem[slot]

    def _wait(self, eng, src, val):
        if val <= 0:
            return
        if self.implied[eng].get(src, 0) >= val:
            return
        if src == eng and val >= self.cnt.get(eng, 0) + 1:
            raise RuntimeError("self-wait on future count")
        self.implied[eng][src] = val
        sem = self.sems[src] if src in self.sems else self.dsem[src]
        self.prog[eng].append(lambda E, sem=sem, val=val: E.wait_ge(sem, val))

    def _dep(self, eng, key, write):
        rec = self.bufs.get(key)
        if rec is None:
            return
        deps = []
        if rec['w'] is not None:
            deps.append(rec['w'])
        if write:
            deps.extend((e, c, i) for e, (c, i) in rec['r'].items())
        for (e, c, i) in deps:
            if c <= 0:
                raise RuntimeError(f"dep on un-fenced work of {e} for {key}")
            self._wait(eng, e, c)

    def _record(self, eng, reads, writes, my_icnt):
        for k in reads:
            rec = self.bufs.setdefault(k, {'w': None, 'r': {}})
            rec['r'][eng] = (self.cnt[eng], my_icnt)
        for k in writes:
            rec = self.bufs.setdefault(k, {'w': None, 'r': {}})
            rec['w'] = (eng, self.cnt[eng], my_icnt)
            rec['r'] = {}

    def op(self, eng, fn, reads=(), writes=()):
        for k in reads:
            self._dep(eng, k, False)
        for k in writes:
            self._dep(eng, k, True)
        self.icount[eng] += 1
        my_icnt = self.icount[eng]
        self.cnt[eng] += 1
        sem = self.sems[eng]
        self.prog[eng].append(lambda E, fn=fn, sem=sem: fn(E).then_inc(sem, 1))
        self._record(eng, reads, writes, my_icnt)
        return self.cnt[eng]

    def mms(self, fns, reads=(), writes=(), claim=(), publish=()):
        eng = "pe"
        for k in reads:
            self._dep(eng, k, False)
        for k in claim:
            self._dep(eng, k, True)
        for k in writes:
            self._dep(eng, k, True)
        self.icount[eng] += len(fns)
        my_icnt = self.icount[eng]
        self.cnt[eng] += 1
        sem = self.sems[eng]
        for fn in fns[:-1]:
            self.prog[eng].append(lambda E, fn=fn: fn(E))
        self.prog[eng].append(lambda E, fn=fns[-1], sem=sem: fn(E).then_inc(sem, 1))
        self._record(eng, reads, tuple(writes) + tuple(claim) + tuple(publish), my_icnt)
        return self.cnt[eng]

    def dma(self, queue, out, in_, reads=(), writes=(), slot=None):
        if slot is None:
            slot = ("w:" + writes[0]) if writes else ("r:" + reads[0])
        sem = self._slot_sem(slot)
        if self.dcnt[slot] > 0:
            self._wait(queue, slot, self.dcnt[slot])
        for k in reads:
            self._dep(queue, k, False)
        for k in writes:
            self._dep(queue, k, True)
        self.icount[queue] += 1
        my_icnt = self.icount[queue]
        self.dcnt[slot] += 16
        cnt = self.dcnt[slot]
        self.prog[queue].append(
            lambda E, out=out, in_=in_, sem=sem, :
                E.dma_start(out=out, in_=in_).then_inc(sem, 16))
        for k in reads:
            rec = self.bufs.setdefault(k, {'w': None, 'r': {}})
            rec['r'][slot] = (cnt, my_icnt)
        for k in writes:
            rec = self.bufs.setdefault(k, {'w': None, 'r': {}})
            rec['w'] = (slot, cnt, my_icnt)
            rec['r'] = {}
        return cnt

    def barrier_all(self):
        snap = dict(self.cnt)
        dsnap = dict(self.dcnt)
        for e in self.ENGINES:
            for e2 in self.ENGINES:
                self._wait(e, e2, snap[e2])
            for s, v in dsnap.items():
                self._wait(e, s, v)
        self.bufs = {}

    def finalize(self, block):
        progs = self.prog

        @block.tensor
        def _(E):
            for f in progs["pe"]:
                f(E)

        @block.vector
        def _(E):
            for f in progs["dve"]:
                f(E)

        @block.scalar
        def _(E):
            for f in progs["act"]:
                f(E)

        @block.sync
        def _(E):
            for f in progs["sp"]:
                f(E)

        @block.gpsimd
        def _(E):
            for f in progs["pool"]:
                f(E)


class Arena:
    def __init__(self, nc, es, name, cols):
        self.t = es.enter_context(nc.sbuf_tensor(name, [128, cols], F32))
        self.off = 0
        self.marks = []

    def alloc(self, cols):
        a = self.off
        self.off += cols
        assert self.off <= self.t.shape[1], f"SBUF arena overflow: {self.off}"
        return a

    def v(self, off, cols, p0=0, p1=128):
        return self.t[p0:p1, off:off + cols]

    def push(self):
        self.marks.append(self.off)

    def pop(self):
        self.off = self.marks.pop()


def build_program(debug=False, stages=("A", "B", "CC", "C1", "C2")):
    nc = bass.Bass("TRN2", target_bir_lowering=False, debug=False)
    es = ExitStack()

    def din(name, shape):
        return nc.dram_tensor(name, shape, F32, kind="ExternalInput").ap()

    x_pre = din("x_pre", [T_ALL, D])
    WguT = din("WguT", [D, 2 * INTER])
    WdT = din("WdT", [INTER, D])
    W_in = din("W_in", [D, PROJ])
    W_out = din("W_out", [D, D])
    b_in_t = din("b_in_t", [128, 33])
    conv_w_t = din("conv_w_t", [128, NCC * 4])
    conv_b_t = din("conv_b_t", [128, NCC])
    d_rep_t = din("d_rep_t", [128, DC])
    d_rep_row = din("d_rep_row", [1, D])
    norm_w_t = din("norm_w_t", [128, DC])
    a_log = din("a_log", [32, 1])
    dt_bias_t = din("dt_bias_t", [32, 1])
    consts = din("consts", [128, 3 * 128 + 3])
    gates_t = din("gates_t", [128, 2])

    out_t = nc.dram_tensor("out", [T_OWN, D], F32, kind="ExternalOutput").ap()
    skind = "ExternalOutput" if debug else "Internal"
    x1_s = nc.dram_tensor("x1_s", [T_ALL, D], F32, kind=skind).ap()
    y0_s = nc.dram_tensor("y0_s", [T_OWN, D], F32, kind=skind).ap()
    gs_s = nc.dram_tensor("gs_s", [DC, 128, T_OWN], F32, kind=skind).ap()
    x2_s = nc.dram_tensor("x2_s", [T_OWN, D], F32, kind=skind).ap()
    WguT_h = nc.dram_tensor("WguT_h", [D, 2 * INTER], BF16).ap()
    WguT_l = nc.dram_tensor("WguT_l", [D, 2 * INTER], BF16).ap()
    WdT_h = nc.dram_tensor("WdT_h", [INTER, D], BF16).ap()
    WdT_l = nc.dram_tensor("WdT_l", [INTER, D], BF16).ap()
    W_in_h = nc.dram_tensor("W_in_h", [D, PROJ], BF16).ap()
    W_in_l = nc.dram_tensor("W_in_l", [D, PROJ], BF16).ap()
    W_out_h = nc.dram_tensor("W_out_h", [D, D], BF16).ap()
    W_out_l = nc.dram_tensor("W_out_l", [D, D], BF16).ap()
    cc_in = nc.dram_tensor("cc_in", [N, H * P], F32).ap()
    cc_out = nc.dram_tensor("cc_out", [N, H * P], F32).ap()
    hin_dbg = nc.dram_tensor("hin_dbg", [N, H * P], F32, kind=skind).ap() if debug else None
    if debug:
        d_stfin = nc.dram_tensor("d_stfin", [N, H * P], F32, kind=skind).ap()
        d_AS = nc.dram_tensor("d_AS", [128, 32], F32, kind=skind).ap()
        d_dtf = nc.dram_tensor("d_dtf", [32, TT_B], F32, kind=skind).ap()
        d_xs0 = nc.dram_tensor("d_xs0", [128, 128], F32, kind=skind).ap()
        d_b = nc.dram_tensor("d_b", [32, TT_B], F32, kind=skind).ap()
        d_c = nc.dram_tensor("d_c", [32, TT_B], F32, kind=skind).ap()
        d_g0 = nc.dram_tensor("d_g0", [128, 128], F32, kind=skind).ap()
        d_msb = nc.dram_tensor("d_msb", [128, 128], F32, kind=skind).ap()
        d_esb = nc.dram_tensor("d_esb", [128, 128], F32, kind=skind).ap()
        d_state = nc.dram_tensor("d_state", [N, H * P], F32, kind=skind).ap()
        d_wtm = nc.dram_tensor("d_wtm", [128, 32], F32, kind=skind).ap()
        d_db = nc.dram_tensor("d_db", [N, H * P], F32, kind=skind).ap()

    b = Builder(nc)
    b.make_sems(es)
    ar = Arena(nc, es, "arena", 53152)

    o_ident = ar.alloc(128)
    o_tri = ar.alloc(128)
    o_mask = ar.alloc(128)
    o_ones = ar.alloc(1)
    o_eps = ar.alloc(2)
    o_gates = ar.alloc(2)
    o_bin = ar.alloc(33)
    o_convw = ar.alloc(NCC * 4)
    o_convb = ar.alloc(NCC)
    o_drep = ar.alloc(DC)
    o_normw = ar.alloc(DC)
    o_alog = ar.alloc(1)
    o_dtb = ar.alloc(1)
    o_state = ar.alloc(H * P)
    o_hinit = ar.alloc(H * P)
    o_lc = ar.alloc(1)
    o_ctm = ar.alloc(NCH * 32)
    o_ftm = ar.alloc(NCH * 32)
    o_tail = ar.alloc(NCC * 3)
    o_drepb = ar.alloc(D)

    ident = ar.v(o_ident, 128)
    tri = ar.v(o_tri, 128)

    def ones_lhsT(m):
        return ar.t[0:1, o_ones:o_ones + 1].broadcast_to([1, m])

    def ones_col():
        return ar.t[:, o_ones:o_ones + 1]

    ps_big = es.enter_context(nc.psum_tensor("psbig", [128, 2048], F32))
    pb = [es.enter_context(nc.psum_tensor(f"pb{i}", [128, 512], F32)) for i in range(4)]

    def psv(j, c0, cols, p0=0, p1=128):
        if j < 4:
            return ps_big[p0:p1, j * 512 + c0: j * 512 + c0 + cols]
        return pb[j - 4][p0:p1, c0:c0 + cols]

    BK = [f"bank{j}" for j in range(8)]

    b.dma("sp", ar.v(o_ident, 387), consts[:, :],
          writes=["c_ident", "c_tri", "c_mask", "c_ones", "c_eps"])
    nc.const_aps.register(F32, EPS_LN, ar.t[:, o_eps:o_eps + 1])
    nc.const_aps.register(F32, EPS_RMS, ar.t[:, o_eps + 1:o_eps + 2])
    b.dma("sp", ar.v(o_gates, 2), gates_t[:, :], writes=["c_gates"])
    b.dma("sp", ar.v(o_bin, 33), b_in_t[:, :], writes=["c_bin"])
    b.dma("sp", ar.v(o_convw, NCC * 4), conv_w_t[:, :], writes=["c_convw"])
    b.dma("sp", ar.v(o_convb, NCC), conv_b_t[:, :], writes=["c_convb"])
    b.dma("sp", ar.v(o_drep, DC), d_rep_t[:, :], writes=["c_drep"])
    b.dma("sp", ar.v(o_normw, DC), norm_w_t[:, :], writes=["c_normw"])
    b.dma("sp", ar.v(o_alog, 1, 0, 32), a_log[:, :], writes=["c_alog"])
    b.dma("sp", ar.v(o_dtb, 1, 0, 32), dt_bias_t[:, :], writes=["c_dtb"])
    b.op("act", lambda E: E.activation(ar.v(o_alog, 1, 0, 32), ar.v(o_alog, 1, 0, 32), AF.Exp),
         reads=["c_alog"], writes=["c_alog"])
    b.op("dve", lambda E: E.tensor_scalar_mul(ar.v(o_alog, 1, 0, 32), ar.v(o_alog, 1, 0, 32), -1.0),
         reads=["c_alog"], writes=["c_alog"])
    b.op("dve", lambda E: E.memset(ar.v(o_state, H * P, 0, 32), 0.0), writes=["state"])
    b.op("dve", lambda E: E.memset(ar.v(o_lc, 1, 0, 32), 0.0), writes=["lc"])
    b.op("dve", lambda E: E.memset(ar.v(o_tail, NCC * 3), 0.0), writes=["tail"])
    b.dma("sp", ar.t[0:1, o_drepb:o_drepb + D], d_rep_row[:, :], writes=["dreprow"])
    for q in range(4):
        b.mms([lambda E, q=q: E.matmul(pb[3][:, 0:512], ones_lhsT(128),
                                       ar.t[0:1, o_drepb + q * 512:o_drepb + (q + 1) * 512],
                                       start=True, stop=True)],
              reads=["dreprow", "c_ones"], writes=["bank7"])
        b.op("dve", lambda E, q=q: E.tensor_copy(ar.v(o_drepb + q * 512, 512), pb[3][:, 0:512]),
             reads=["bank7"], writes=["drepb"])

    def emit_ln_chunk(o_xc, o_stat, o_sqt, kx, kst, ksq="lnsq"):
        xc = ar.v(o_xc, D)
        s_m = ar.t[:, o_stat:o_stat + 1]
        s_v = ar.t[:, o_stat + 1:o_stat + 2]
        s_t = ar.t[:, o_stat + 2:o_stat + 3]
        sq = ar.v(o_sqt, D)
        b.op("act", lambda E: E.activation(sq, xc, AF.Square, accum_out=s_v),
             reads=[kx], writes=[ksq, kst + "v"])
        b.op("dve", lambda E: E.reduce_sum(s_m, xc, axis=AXX),
             reads=[kx], writes=[kst + "m"])
        b.op("act", lambda E: E.mul(s_m, s_m, 1.0 / D),
             reads=[kst + "m"], writes=[kst + "m"])
        b.op("act", lambda E: E.mul(s_v, s_v, 1.0 / D),
             reads=[kst + "v"], writes=[kst + "v"])
        b.op("act", lambda E: E.activation(s_t, s_m, AF.Square),
             reads=[kst + "m"], writes=[kst + "t"])
        b.op("dve", lambda E: E.tensor_sub(s_v, s_v, s_t),
             reads=[kst + "v", kst + "t"], writes=[kst + "v"])
        b.op("act", lambda E: E.activation(s_v, s_v, AF.Sqrt, bias=EPS_LN),
             reads=[kst + "v", "c_eps"], writes=[kst + "v"])
        b.op("dve", lambda E: E.reciprocal(s_v, s_v),
             reads=[kst + "v"], writes=[kst + "v"])
        b.op("dve", lambda E: E.tensor_scalar(sq, xc, s_m, s_v,
                                              op0=ALU.subtract, op1=ALU.mult),
             reads=[kst + "m", kst + "v", kx], writes=[ksq])

    def emit_tp(src_ap, dst_ap, kin, kout, bank, c0, src_p, src_c):
        pvw = psv(bank, c0, src_p, 0, src_c)
        id_ap = ar.t[0:src_p, o_ident:o_ident + src_p]
        b.op("pe", lambda E: E.transpose(pvw, src_ap, id_ap),
             reads=[kin, "c_ident"], writes=[BK[bank]])
        b.op("dve", lambda E: E.tensor_copy(dst_ap, pvw),
             reads=[BK[bank]], writes=[kout])

    def emit_ffn_stage(t_total, src_ap_fn, dst_ap_fn, split_first):
        ar.push()
        o_xtm = ar.alloc(3 * D)
        o_xhh = ar.alloc(DC * TT_A // 2)
        o_xhl = ar.alloc(DC * TT_A // 2)
        o_hh = ar.alloc(IC * TT_A // 2)
        o_hl = ar.alloc(IC * TT_A // 2)
        NP = 3
        o_ph = ar.alloc(NP * 512)
        o_pl = ar.alloc(NP * 512)
        o_pf = ar.alloc(1024)
        o_gtmp = ar.alloc(8 * TT_A)
        o_sqt = o_gtmp
        o_ptmp = ar.alloc(TT_A)
        o_dn = o_ptmp
        o_stat = ar.alloc(12)
        pan_i = [0]

        def bfv(off_f32, col0_bf, ncol_bf):
            return ar.t[:, off_f32:off_f32 + 1].bitcast(BF16).tensor[
                :, off_f32 * 2 + col0_bf: off_f32 * 2 + col0_bf + ncol_bf] \
                if False else ar.t[:, off_f32: off_f32 + (col0_bf + ncol_bf + 1) // 2] \
                .bitcast(BF16)[:, col0_bf:col0_bf + ncol_bf]

        def panel(dram_f, dram_h, dram_l, wk, first, w=1024):
            pi = pan_i[0] % NP
            pan_i[0] += 1
            hi = ar.v(o_ph + pi * 512, (w + 1) // 2).bitcast(BF16)[:, :w]
            lo = ar.v(o_pl + pi * 512, (w + 1) // 2).bitcast(BF16)[:, :w]
            if first:
                pv = ar.v(o_pf, w)
                b.dma("sp", pv, dram_f, writes=["pf0"])
                b.op("act", lambda E, hi=hi, pv=pv: E.activation(hi, pv, AF.Copy),
                     reads=["pf0"], writes=[f"ph{pi}"])
                b.op("dve", lambda E, lo=lo, pv=pv, hi=hi:
                     E.tensor_tensor(lo, pv, hi, op=ALU.subtract),
                     reads=["pf0", f"ph{pi}"], writes=[f"pl{pi}"])
                b.dma("pool", dram_h, hi, reads=[f"ph{pi}"], writes=[wk], slot="wch")
                b.dma("pool", dram_l, lo, reads=[f"pl{pi}"], writes=[wk], slot="wcl")
            else:
                b.dma("sp", hi, dram_h, writes=[f"ph{pi}"], reads=[wk])
                b.dma("sp", lo, dram_l, writes=[f"pl{pi}"], reads=[wk])
            return hi, lo, pi

        def mm3(j, tt, whi, wlo, xhi, xlo, first, last):
            return [
                (lambda E, j=j, tt=tt, whi=whi, xhi=xhi, first=first:
                 E.matmul(psv(j, 0, tt), whi, xhi, start=first, stop=False)),
                (lambda E, j=j, tt=tt, whi=whi, xlo=xlo:
                 E.matmul(psv(j, 0, tt), whi, xlo, start=False, stop=False)),
                (lambda E, j=j, tt=tt, wlo=wlo, xhi=xhi, last=last:
                 E.matmul(psv(j, 0, tt), wlo, xhi, start=False, stop=last)),
            ]

        for ti, (t0, tt) in enumerate(_tiles(t_total, TT_A)):
            ntc = tt // 128
            for c in range(ntc):
                kx = f"xtm{c}"
                b.dma("pool", ar.v(o_xtm + c * D, D), src_ap_fn(t0 + c * 128, 128),
                      writes=[kx])
                emit_ln_chunk(o_xtm + c * D, o_stat + 4 * c, o_sqt, kx, f"ln{c}", ksq="scr")
                for f in range(DC):
                    pvw = psv(7, 0, 128)
                    b.op("pe", lambda E, f=f, c=c, pvw=pvw:
                         E.transpose(pvw, ar.v(o_sqt + f * 128, 128), ident),
                         reads=["scr", "c_ident"], writes=[BK[7]])
                    hi = ar.v(o_xhh + f * TT_A // 2, TT_A // 2).bitcast(BF16)[:, c * 128:c * 128 + 128]
                    lo = ar.v(o_xhl + f * TT_A // 2, TT_A // 2).bitcast(BF16)[:, c * 128:c * 128 + 128]
                    b.op("act", lambda E, hi=hi, pvw=pvw: E.activation(hi, pvw, AF.Copy),
                         reads=[BK[7]], writes=[f"xhh{f}"])
                    b.op("dve", lambda E, lo=lo, pvw=pvw, hi=hi:
                         E.tensor_tensor(lo, pvw, hi, op=ALU.subtract),
                         reads=[BK[7], f"xhh{f}"], writes=[f"xhl{f}"])
            for g in range(8):
                for mg in (g, 8 + g):
                    for k in range(DC):
                        whi, wlo, pi = panel(WguT[k * 128:(k + 1) * 128, mg * 1024:(mg + 1) * 1024],
                                             WguT_h[k * 128:(k + 1) * 128, mg * 1024:(mg + 1) * 1024],
                                             WguT_l[k * 128:(k + 1) * 128, mg * 1024:(mg + 1) * 1024],
                                             "splitWgu%d_%d" % (k, mg), split_first and ti == 0)
                        xhi = ar.v(o_xhh + k * TT_A // 2, TT_A // 2).bitcast(BF16)[:, 0:tt]
                        xlo = ar.v(o_xhl + k * TT_A // 2, TT_A // 2).bitcast(BF16)[:, 0:tt]
                        if k == 0:
                            for hf in range(2):
                                fns = []
                                for j in range(hf * 4, hf * 4 + 4):
                                    fns += mm3(j, tt, whi[:, j * 128:(j + 1) * 128],
                                               wlo[:, j * 128:(j + 1) * 128], xhi, xlo,
                                               True, False)
                                b.mms(fns, reads=[f"ph{pi}", f"pl{pi}", f"xhh{k}", f"xhl{k}"],
                                      claim=BK[hf * 4:hf * 4 + 4])
                        else:
                            fns = []
                            for j in range(8):
                                fns += mm3(j, tt, whi[:, j * 128:(j + 1) * 128],
                                           wlo[:, j * 128:(j + 1) * 128], xhi, xlo,
                                           False, k == DC - 1)
                            b.mms(fns, reads=[f"ph{pi}", f"pl{pi}", f"xhh{k}", f"xhl{k}"],
                                  publish=BK if k == DC - 1 else ())
                    for j in range(8):
                        if mg < 8:
                            gv = ar.v(o_gtmp + j * TT_A, tt)
                            b.op("act", lambda E, gv=gv, j=j, tt=tt:
                                 E.activation(gv, psv(j, 0, tt), AF.Silu),
                                 reads=[BK[j]], writes=["scr"])
                        else:
                            hj = (mg - 8) * 8 + j
                            pt = ar.v(o_ptmp, tt)
                            hhv = ar.v(o_hh + hj * TT_A // 2, TT_A // 2).bitcast(BF16)[:, 0:tt]
                            hlv = ar.v(o_hl + hj * TT_A // 2, TT_A // 2).bitcast(BF16)[:, 0:tt]
                            b.op("dve", lambda E, pt=pt, j=j, tt=tt:
                                 E.tensor_tensor(pt, ar.v(o_gtmp + j * TT_A, tt),
                                                 psv(j, 0, tt), op=ALU.mult),
                                 reads=[BK[j], "scr"], writes=["dnp"])
                            b.op("act", lambda E, hhv=hhv, pt=pt: E.activation(hhv, pt, AF.Copy),
                                 reads=["dnp"], writes=[f"hh{hj}"])
                            b.op("dve", lambda E, hlv=hlv, pt=pt, hhv=hhv:
                                 E.tensor_tensor(hlv, pt, hhv, op=ALU.subtract),
                                 reads=["dnp", f"hh{hj}"], writes=[f"hl{hj}"])
            for mg in range(2):
                for k in range(IC):
                    whi, wlo, pi = panel(WdT[k * 128:(k + 1) * 128, mg * 1024:(mg + 1) * 1024],
                                         WdT_h[k * 128:(k + 1) * 128, mg * 1024:(mg + 1) * 1024],
                                         WdT_l[k * 128:(k + 1) * 128, mg * 1024:(mg + 1) * 1024],
                                         "splitWd%d_%d" % (k, mg), split_first and ti == 0)
                    hhv = ar.v(o_hh + k * TT_A // 2, TT_A // 2).bitcast(BF16)[:, 0:tt]
                    hlv = ar.v(o_hl + k * TT_A // 2, TT_A // 2).bitcast(BF16)[:, 0:tt]
                    if k == 0:
                        for hf in range(2):
                            fns = []
                            for j in range(hf * 4, hf * 4 + 4):
                                fns += mm3(j, tt, whi[:, j * 128:(j + 1) * 128],
                                           wlo[:, j * 128:(j + 1) * 128], hhv, hlv,
                                           True, False)
                            b.mms(fns, reads=[f"ph{pi}", f"pl{pi}", f"hh{k}", f"hl{k}"],
                                  claim=BK[hf * 4:hf * 4 + 4])
                    else:
                        fns = []
                        for j in range(8):
                            fns += mm3(j, tt, whi[:, j * 128:(j + 1) * 128],
                                       wlo[:, j * 128:(j + 1) * 128], hhv, hlv,
                                       False, k == IC - 1)
                        b.mms(fns, reads=[f"ph{pi}", f"pl{pi}", f"hh{k}", f"hl{k}"],
                              publish=BK if k == IC - 1 else ())
                for j in range(8):
                    f = mg * 8 + j
                    dn = ar.v(o_dn, tt)
                    b.op("act", lambda E, dn=dn, j=j, tt=tt:
                         E.activation(dn, psv(j, 0, tt), AF.Copy, scale=0.5),
                         reads=[BK[j]], writes=["dnp"])
                    for c in range(ntc):
                        b.op("pe", lambda E, c=c, j=j:
                             E.transpose(psv(j, 0, 128), ar.v(o_dn + c * 128, 128), ident),
                             reads=["dnp", "c_ident"], writes=[BK[j]])
                        xsl = ar.v(o_xtm + c * D + f * 128, 128)
                        b.op("dve", lambda E, xsl=xsl, j=j:
                             E.tensor_add(xsl, xsl, psv(j, 0, 128)),
                             reads=[BK[j], f"xtm{c}"], writes=[f"xtm{c}"])
            for c in range(ntc):
                b.dma("pool", dst_ap_fn(t0 + c * 128, 128), ar.v(o_xtm + c * D, D),
                      reads=[f"xtm{c}"])
        ar.pop()

    if "A" in stages:
        emit_ffn_stage(T_ALL,
                       lambda t, n: x_pre[t:t + n, :],
                       lambda t, n: x1_s[t:t + n, :], True)
        b.barrier_all()

    if "B" in stages:
        ar.push()
        o_xtm = ar.alloc(D)
        o_xhh = ar.alloc(DC * TT_B // 2)
        o_xhl = ar.alloc(DC * TT_B // 2)
        NPB = 6
        o_pph = ar.alloc(NPB * 512)
        o_ppl = ar.alloc(NPB * 512)
        o_pfB = ar.alloc(1024)
        o_stat = ar.alloc(12)
        o_sqt = ar.alloc(D)
        o_hcat = ar.alloc(NCC * (TT_B + 3))
        o_xs = ar.alloc(NCC * TT_B)
        o_gsb = ar.alloc(2 * TT_B)
        o_p32 = ar.alloc(TT_B)
        o_bf0 = ar.alloc(TT_B)
        o_cf0 = ar.alloc(TT_B)
        o_ca1 = ar.alloc(TT_B)
        o_ca2 = ar.alloc(TT_B)
        o_dtraw = ar.alloc(TT_B)
        o_dts1 = ar.alloc(TT_B)
        o_dts2 = ar.alloc(TT_B)
        o_dtf = ar.alloc(TT_B)
        o_dtA = ar.alloc(TT_B)
        o_xstm = ar.alloc(D)
        o_dtAtm = ar.alloc(32)
        o_dttm = ar.alloc(32)
        o_AStm = ar.alloc(32)
        o_expAS = ar.alloc(32)
        o_ASf = ar.alloc(128)
        o_Ff = ar.alloc(128)
        o_wsb = ar.alloc(128)
        o_Wtm = ar.alloc(32)
        o_btm = ar.alloc(32)
        o_dcol = ar.alloc(1)
        o_drow = ar.alloc(32)
        o_dbsb = ar.alloc(H * P)
        o_g0 = ar.alloc(128)
        o_esb = ar.alloc(2 * 512)
        o_msb = ar.alloc(2 * 512)
        o_cfh = ar.alloc(2 * 64)
        o_cfsb = ar.alloc(4 * 128)
        o_xwh = ar.alloc(2 * 256)
        o_y0tm = ar.alloc(D)
        pan_i = [0]

        def panelB(dram_f, dram_h, dram_l, wk, first, w=1024):
            pi = pan_i[0] % NPB
            pan_i[0] += 1
            hi = ar.v(o_pph + pi * 512, (w + 1) // 2).bitcast(BF16)[:, :w]
            lo = ar.v(o_ppl + pi * 512, (w + 1) // 2).bitcast(BF16)[:, :w]
            if first:
                pv = ar.v(o_pfB, w)
                b.dma("sp", pv, dram_f, writes=["pf0"])
                b.op("act", lambda E, hi=hi, pv=pv: E.activation(hi, pv, AF.Copy),
                     reads=["pf0"], writes=[f"ph{pi}"])
                b.op("dve", lambda E, lo=lo, pv=pv, hi=hi:
                     E.tensor_tensor(lo, pv, hi, op=ALU.subtract),
                     reads=["pf0", f"ph{pi}"], writes=[f"pl{pi}"])
                b.dma("pool", dram_h, hi, reads=[f"ph{pi}"], writes=[wk], slot="wch")
                b.dma("pool", dram_l, lo, reads=[f"pl{pi}"], writes=[wk], slot="wcl")
            else:
                b.dma("sp", hi, dram_h, writes=[f"ph{pi}"], reads=[wk])
                b.dma("sp", lo, dram_l, writes=[f"pl{pi}"], reads=[wk])
            return hi, lo, pi

        PGRP = [(0, 8, 1024), (1, 8, 1024), (2, 8, 1024), (3, 8, 1024), (4, 1, 96)]

        for ti, (t0, tt) in enumerate(_tiles(T_ALL, TT_B)):
            ntc = tt // 128
            for c in range(ntc):
                kx = "xtm0"
                b.dma("pool", ar.v(o_xtm, D),
                      x1_s[t0 + c * 128:t0 + c * 128 + 128, :], writes=[kx])
                emit_ln_chunk(o_xtm, o_stat + 4 * c, o_sqt, kx, f"ln{c}")
                for f in range(DC):
                    pvw = psv(7, 0, 128)
                    b.op("pe", lambda E, f=f, pvw=pvw:
                         E.transpose(pvw, ar.v(o_sqt + f * 128, 128), ident),
                         reads=["lnsq", "c_ident"], writes=[BK[7]])
                    hi = ar.v(o_xhh + f * TT_B // 2, TT_B // 2).bitcast(BF16)[:, c * 128:c * 128 + 128]
                    lo = ar.v(o_xhl + f * TT_B // 2, TT_B // 2).bitcast(BF16)[:, c * 128:c * 128 + 128]
                    b.op("act", lambda E, hi=hi, pvw=pvw: E.activation(hi, pvw, AF.Copy),
                         reads=[BK[7]], writes=[f"xhh{f}"])
                    b.op("dve", lambda E, lo=lo, pvw=pvw, hi=hi:
                         E.tensor_tensor(lo, pvw, hi, op=ALU.subtract),
                         reads=[BK[7], f"xhh{f}"], writes=[f"xhl{f}"])
            for (mg, nmt, gcols) in PGRP:
                for k in range(DC):
                    whi, wlo, pi = panelB(
                        W_in[k * 128:(k + 1) * 128, mg * 1024:mg * 1024 + gcols],
                        W_in_h[k * 128:(k + 1) * 128, mg * 1024:mg * 1024 + gcols],
                        W_in_l[k * 128:(k + 1) * 128, mg * 1024:mg * 1024 + gcols],
                        "splitWin%d_%d" % (k, mg), ti == 0, gcols)
                    xhi = ar.v(o_xhh + k * TT_B // 2, TT_B // 2).bitcast(BF16)[:, 0:tt]
                    xlo = ar.v(o_xhl + k * TT_B // 2, TT_B // 2).bitcast(BF16)[:, 0:tt]
                    def pfns(j0, j1, k=k, tt=tt, whi=whi, wlo=wlo, xhi=xhi, xlo=xlo, gcols=gcols):
                        fns = []
                        for j in range(j0, j1):
                            w = min(128, gcols - j * 128)
                            fns += [
                                (lambda E, j=j, tt=tt, w=w, whi=whi, xhi=xhi, k=k:
                                 E.matmul(psv(j, 0, tt, 0, w), whi[:, j * 128:j * 128 + w],
                                          xhi, start=(k == 0), stop=False)),
                                (lambda E, j=j, tt=tt, w=w, whi=whi, xlo=xlo:
                                 E.matmul(psv(j, 0, tt, 0, w), whi[:, j * 128:j * 128 + w],
                                          xlo, start=False, stop=False)),
                                (lambda E, j=j, tt=tt, w=w, wlo=wlo, xhi=xhi, k=k:
                                 E.matmul(psv(j, 0, tt, 0, w), wlo[:, j * 128:j * 128 + w],
                                          xhi, start=False, stop=(k == DC - 1))),
                            ]
                        return fns
                    if k == 0 and nmt == 8:
                        b.mms(pfns(0, 4), reads=[f"ph{pi}", f"pl{pi}", f"xhh{k}", f"xhl{k}"],
                              claim=BK[:4])
                        b.mms(pfns(4, 8), reads=[f"ph{pi}", f"pl{pi}", f"xhh{k}", f"xhl{k}"],
                              claim=BK[4:8])
                    else:
                        b.mms(pfns(0, nmt), reads=[f"ph{pi}", f"pl{pi}", f"xhh{k}", f"xhl{k}"],
                              claim=BK[:nmt] if k == 0 else (),
                              publish=BK[:nmt] if k == DC - 1 else ())
                for j in range(nmt):
                    pc = mg * 8 + j
                    bias = ar.t[:, o_bin + pc:o_bin + pc + 1]
                    if pc < DC:
                        gv = ar.v(o_gsb + (pc % 2) * TT_B, tt)
                        b.op("act", lambda E, gv=gv, j=j, bias=bias, tt=tt:
                             E.activation(gv, psv(j, 0, tt), AF.Silu, bias=bias),
                             reads=[BK[j], "c_bin"], writes=[f"gs{pc % 2}"])
                        own_lo = max(0, T_HALO - t0)
                        if own_lo < tt:
                            ot0 = t0 + own_lo - T_HALO
                            b.dma("pool", gs_s[pc, :, ot0:ot0 + (tt - own_lo)],
                                  ar.v(o_gsb + (pc % 2) * TT_B + own_lo, tt - own_lo),
                                  reads=[f"gs{pc % 2}"])
                    elif pc < 32:
                        cc = pc - DC
                        hv = ar.v(o_hcat + cc * (TT_B + 3) + 3, tt)
                        b.op("dve", lambda E, hv=hv, j=j, bias=bias, tt=tt:
                             E.tensor_scalar(hv, psv(j, 0, tt), bias, None, op0=ALU.add),
                             reads=[BK[j], "c_bin"], writes=[f"hc{cc}"])
                    else:
                        pv96 = ar.v(o_p32, tt, 0, 96)
                        b.op("dve", lambda E, pv96=pv96, j=j, tt=tt:
                             E.tensor_scalar(pv96, psv(j, 0, tt, 0, 96),
                                             ar.t[0:96, o_bin + 32:o_bin + 33], None, op0=ALU.add),
                             reads=[BK[j], "c_bin"], writes=["p32"])
            b.op("dve", lambda E, tt=tt:
                 E.tensor_copy(ar.v(o_hcat + 16 * (TT_B + 3) + 3, tt, 0, 64),
                               ar.v(o_p32, tt, 0, 64)),
                 reads=["p32"], writes=["hc16"])
            b.dma("pool", ar.v(o_dtraw, tt, 0, 32), ar.v(o_p32, tt, 64, 96),
                  reads=["p32"], writes=["dtraw"])
            if ti == 0:
                for cc in range(NCC):
                    nrow = 128 if cc < 16 else 64
                    hv = ar.v(o_hcat + cc * (TT_B + 3) + 3, 128, 0, nrow)
                    b.op("dve", lambda E, hv=hv, nrow=nrow:
                         E.tensor_scalar(hv, hv, ar.t[0:nrow, o_gates:o_gates + 1], None, op0=ALU.mult),
                         reads=[f"hc{cc}", "c_gates"], writes=[f"hc{cc}"])
            for cc in range(NCC):
                nrow = 128 if cc < 16 else 64
                b.op("dve", lambda E, cc=cc, nrow=nrow:
                     E.tensor_copy(ar.v(o_hcat + cc * (TT_B + 3), 3, 0, nrow),
                                   ar.v(o_tail + cc * 3, 3, 0, nrow)),
                     reads=["tail"], writes=[f"hc{cc}"])
            for cc in range(NCC):
                nrow = 128 if cc < 16 else 64
                a1 = ar.v(o_ca1, tt, 0, nrow)
                a2v = ar.v(o_ca2, tt, 0, nrow)
                hcs = [ar.v(o_hcat + cc * (TT_B + 3) + k, tt, 0, nrow) for k in range(4)]
                ws = [ar.t[0:nrow, o_convw + cc * 4 + k:o_convw + cc * 4 + k + 1] for k in range(4)]
                b.op("dve", lambda E, a1=a1, hc0=hcs[0], w0=ws[0]:
                     E.tensor_scalar(a1, hc0, w0, None, op0=ALU.mult),
                     reads=[f"hc{cc}", "c_convw"], writes=["cacc1"])
                b.op("dve", lambda E, a2v=a2v, hc1=hcs[1], w1=ws[1], a1=a1:
                     E.scalar_tensor_tensor(a2v, hc1, w1, a1, op0=ALU.mult, op1=ALU.add),
                     reads=[f"hc{cc}", "c_convw", "cacc1"], writes=["cacc2"])
                b.op("dve", lambda E, a1=a1, hc2=hcs[2], w2=ws[2], a2v=a2v:
                     E.scalar_tensor_tensor(a1, hc2, w2, a2v, op0=ALU.mult, op1=ALU.add),
                     reads=[f"hc{cc}", "c_convw", "cacc2"], writes=["cacc1"])
                b.op("dve", lambda E, a2v=a2v, hc3=hcs[3], w3=ws[3], a1=a1:
                     E.scalar_tensor_tensor(a2v, hc3, w3, a1, op0=ALU.mult, op1=ALU.add),
                     reads=[f"hc{cc}", "c_convw", "cacc1"], writes=["cacc2"])
                xsv = ar.v(o_xs + cc * TT_B, tt, 0, nrow)
                cb = ar.t[0:nrow, o_convb + cc:o_convb + cc + 1]
                b.op("act", lambda E, xsv=xsv, a2v=a2v, cb=cb:
                     E.activation(xsv, a2v, AF.Silu, bias=cb),
                     reads=["cacc2", "c_convb"], writes=[f"xs{cc}"])
            b.op("dve", lambda E, tt=tt: E.tensor_copy(
                ar.v(o_tail, NCC * 3).rearrange("p (c k) -> p c k", k=3),
                ar.t[:, o_hcat:o_hcat + NCC * (TT_B + 3)]
                    .rearrange("p (c q) -> p c q", q=TT_B + 3)[:, :, tt:tt + 3]),
                 reads=[f"hc{cc}" for cc in range(NCC)], writes=["tail"])
            b.dma("pool", ar.v(o_bf0, tt, 0, 32), ar.v(o_xs + 16 * TT_B, tt, 0, 32),
                  reads=["xs16"], writes=["bf0"])
            b.dma("pool", ar.v(o_cf0, tt, 0, 32), ar.v(o_xs + 16 * TT_B, tt, 32, 64),
                  reads=["xs16"], writes=["cf0"])
            dtb = ar.v(o_dtb, 1, 0, 32)
            b.op("act", lambda E, tt=tt: E.activation(ar.v(o_dts1, tt, 0, 32),
                                                      ar.v(o_dtraw, tt, 0, 32), AF.Abs, bias=dtb),
                 reads=["dtraw", "c_dtb"], writes=["dts1"])
            b.op("act", lambda E, tt=tt: E.activation(ar.v(o_dts1, tt, 0, 32),
                                                      ar.v(o_dts1, tt, 0, 32), AF.Exp, scale=-1.0),
                 reads=["dts1"], writes=["dts1"])
            b.op("act", lambda E, tt=tt: E.activation(ar.v(o_dts1, tt, 0, 32),
                                                      ar.v(o_dts1, tt, 0, 32), AF.Ln, bias=1.0),
                 reads=["dts1"], writes=["dts1"])
            b.op("act", lambda E, tt=tt: E.activation(ar.v(o_dts2, tt, 0, 32),
                                                      ar.v(o_dtraw, tt, 0, 32), AF.Relu, bias=dtb),
                 reads=["dtraw", "c_dtb"], writes=["dts2"])
            b.op("dve", lambda E, tt=tt: E.tensor_add(ar.v(o_dtf, tt, 0, 32),
                                                      ar.v(o_dts1, tt, 0, 32),
                                                      ar.v(o_dts2, tt, 0, 32)),
                 reads=["dts1", "dts2"], writes=["dtf"])
            b.op("dve", lambda E, tt=tt: E.tensor_scalar(ar.v(o_dtA, tt, 0, 32),
                                                         ar.v(o_dtf, tt, 0, 32),
                                                         ar.v(o_alog, 1, 0, 32), None, op0=ALU.mult),
                 reads=["dtf", "c_alog"], writes=["dtA"])

            for cj in range(ntc):
                cg = (t0 // 128) + cj - 1
                if cg < 0:
                    continue
                cs = cj * 128
                emit_tp(ar.v(o_dtA + cs, 128, 0, 32), ar.v(o_dtAtm, 32),
                        "dtA", "dtAtm", 7, 0, 32, 128)
                emit_tp(ar.v(o_dtf + cs, 128, 0, 32), ar.v(o_dttm, 32),
                        "dtf", "dttm", 7, 128, 32, 128)
                b.mms([lambda E: E.matmul(psv(7, 256, 32), tri, ar.v(o_dtAtm, 32),
                                          start=True, stop=True)],
                      reads=["c_tri", "dtAtm"], writes=[BK[7]])
                b.op("dve", lambda E: E.tensor_copy(ar.v(o_AStm, 32), psv(7, 256, 32)),
                     reads=[BK[7]], writes=["AStm"])
                emit_tp(ar.v(o_AStm, 32), ar.v(o_ASf, 128, 0, 32),
                        "AStm", "ASf", 7, 288, 128, 32)
                b.op("act", lambda E: E.activation(ar.v(o_expAS, 32), ar.v(o_AStm, 32), AF.Exp),
                     reads=["AStm"], writes=["expAS"])
                b.op("act", lambda E: E.activation(ar.v(o_Ff, 128, 0, 32),
                                                   ar.v(o_ASf, 128, 0, 32), AF.Exp,
                                                   bias=ar.v(o_lc, 1, 0, 32)),
                     reads=["ASf", "lc"], writes=["Ff"])
                emit_tp(ar.v(o_Ff, 128, 0, 32), ar.v(o_ftm + cg * 32, 32),
                        "Ff", "ftm", 7, 0, 32, 128)
                b.op("dve", lambda E: E.tensor_scalar(ar.v(o_wsb, 128, 0, 32),
                                                      ar.v(o_ASf, 128, 0, 32),
                                                      ar.t[0:32, o_ASf + 127:o_ASf + 128],
                                                      None, op0=ALU.subtract),
                     reads=["ASf"], writes=["wsb"])
                b.op("act", lambda E: E.activation(ar.v(o_wsb, 128, 0, 32),
                                                   ar.v(o_wsb, 128, 0, 32), AF.Exp, scale=-1.0),
                     reads=["wsb"], writes=["wsb"])
                b.op("dve", lambda E, cs=cs: E.tensor_mul(ar.v(o_wsb, 128, 0, 32),
                                                          ar.v(o_wsb, 128, 0, 32),
                                                          ar.v(o_dtf + cs, 128, 0, 32)),
                     reads=["wsb", "dtf"], writes=["wsb"])
                emit_tp(ar.v(o_wsb, 128, 0, 32), ar.v(o_Wtm, 32),
                        "wsb", "Wtm", 7, 128, 32, 128)
                b.op("act", lambda E: E.activation(ar.v(o_dcol, 1, 0, 32),
                                                   ar.t[0:32, o_ASf + 127:o_ASf + 128], AF.Exp),
                     reads=["ASf"], writes=["dcol"])
                b.op("dve", lambda E: E.tensor_add(ar.v(o_lc, 1, 0, 32), ar.v(o_lc, 1, 0, 32),
                                                   ar.t[0:32, o_ASf + 127:o_ASf + 128]),
                     reads=["ASf", "Ff", "lc"], writes=["lc"])
                emit_tp(ar.v(o_dcol, 1, 0, 32), ar.t[0:1, o_drow:o_drow + 32],
                        "dcol", "drow", 7, 256, 32, 1)
                b.mms([lambda E: E.matmul(psv(6, 0, 32, 0, 32), ones_lhsT(32),
                                          ar.t[0:1, o_drow:o_drow + 32],
                                          start=True, stop=True)],
                      reads=["drow", "c_ones"], writes=[BK[6]])
                b.op("dve", lambda E: E.tensor_copy(ar.v(o_dbsb, 32, 0, 32),
                                                    psv(6, 0, 32, 0, 32)),
                     reads=[BK[6]], writes=["dbsb0"])
                b.mms([lambda E, cs=cs: E.matmul(psv(6, 0, 128),
                                                 ar.v(o_bf0 + cs, 128, 0, 32),
                                                 ar.v(o_cf0 + cs, 128, 0, 32),
                                                 start=True, stop=True)],
                      reads=["bf0", "cf0"], writes=[BK[6]])
                b.op("dve", lambda E: E.tensor_copy(ar.v(o_g0, 128), psv(6, 0, 128)),
                     reads=[BK[6]], writes=["g0"])
                emit_tp(ar.v(o_cf0 + cs, 128, 0, 32), ar.v(o_ctm + cg * 32, 32),
                        "cf0", "ctm", 7, 0, 32, 128)
                emit_tp(ar.v(o_bf0 + cs, 128, 0, 32), ar.v(o_btm, 32),
                        "bf0", "btm", 7, 128, 32, 128)
                for f in range(DC):
                    emit_tp(ar.v(o_xs + f * TT_B + cs, 128), ar.v(o_xstm + f * 128, 128),
                            f"xs{f}", f"xstm{f}", 7, 256, 128, 128)
                if debug and cg == 0:
                    b.dma("pool", d_AS[:, :], ar.v(o_AStm, 32), reads=["AStm"])
                    b.dma("pool", d_dtf[:, :], ar.v(o_dtf, TT_B, 0, 32), reads=["dtf"])
                    b.dma("pool", d_xs0[:, :], ar.v(o_xs + 0 * TT_B + cs, 128), reads=["xs0"])
                    b.dma("pool", d_b[:, :], ar.v(o_bf0, TT_B, 0, 32), reads=["bf0"])
                    b.dma("pool", d_c[:, :], ar.v(o_cf0, TT_B, 0, 32), reads=["cf0"])
                    b.dma("pool", d_g0[:, :], ar.v(o_g0, 128), reads=["g0"])
                    b.dma("pool", d_wtm[:, :], ar.v(o_Wtm, 32), reads=["Wtm"])
                    b.dma("pool", d_db[:, :], ar.v(o_dbsb, H * P, 0, 32),
                          reads=["dbsb0", "dbsb1", "dbsb2", "dbsb3"])
                for g4 in range(H // 4):
                    h0 = 4 * g4
                    gp = g4 % 2
                    fns = [
                        (lambda E, q=q, h=h0 + q:
                         E.matmul(psv(6, q * 128, 128),
                                  ar.t[0:32, o_ident + h:o_ident + h + 1].broadcast_to([32, 128]),
                                  ar.v(o_ASf, 128, 0, 32), start=True, stop=True))
                        for q in range(4)
                    ]
                    b.mms(fns, reads=["ASf", "c_ident"], writes=[BK[6]])
                    esb4 = ar.v(o_esb + gp * 512, 512)
                    msb4 = ar.v(o_msb + gp * 512, 512)
                    ke, km = f"esb{gp}", f"msb{gp}"
                    b.op("dve", lambda E, esb4=esb4, h0=h0:
                         E.tensor_tensor(esb4, psv(6, 0, 512),
                                         ar.t[:, o_AStm + h0:o_AStm + h0 + 4]
                                         .rearrange("p (h o) -> p h o", o=1)
                                         .broadcast_to([128, 4, 128]), op=ALU.subtract),
                         reads=[BK[6], "AStm"], writes=[ke])
                    b.op("dve", lambda E, esb4=esb4:
                         E.tensor_tensor(esb4, esb4,
                                         ar.v(o_mask, 128)
                                         .rearrange("p (a t) -> p a t", a=1)
                                         .broadcast_to([128, 4, 128]), op=ALU.add),
                         reads=[ke, "c_mask"], writes=[ke])
                    b.op("act", lambda E, msb4=msb4, esb4=esb4:
                         E.activation(msb4, esb4, AF.Exp),
                         reads=[ke], writes=[km])
                    b.op("dve", lambda E, msb4=msb4, h0=h0:
                         E.tensor_tensor(msb4, msb4,
                                         ar.t[:, o_dttm + h0:o_dttm + h0 + 4]
                                         .rearrange("p (h o) -> p h o", o=1)
                                         .broadcast_to([128, 4, 128]), op=ALU.mult),
                         reads=[km, "dttm"], writes=[km])
                    b.op("dve", lambda E, msb4=msb4:
                         E.tensor_tensor(msb4, msb4,
                                         ar.v(o_g0, 128)
                                         .rearrange("p (a t) -> p a t", a=1)
                                         .broadcast_to([128, 4, 128]), op=ALU.mult),
                         reads=[km, "g0"], writes=[km])
                    for q in range(4):
                        h = h0 + q
                        a8 = h % 8
                        yb = 4 + ((h // 8) % 2)
                        b.mms([(lambda E, h=h, yb=yb, a8=a8, msb4=msb4, q=q:
                                E.matmul(psv(yb, a8 * 64, 64), msb4[:, q * 128:(q + 1) * 128],
                                         ar.t[:, o_xstm + h * 64:o_xstm + (h + 1) * 64],
                                         start=(a8 == 0), stop=(a8 == 7)))],
                              reads=[f"xstm{h // 2}", km],
                              claim=[BK[yb]] if a8 == 0 else (),
                              publish=[BK[yb]] if a8 == 7 else ())
                        b.mms([(lambda E, h=h, a8=a8, cs=cs:
                                E.matmul(psv(7, a8 * 64, 64),
                                         ar.v(o_cf0 + cs, 128, 0, 32),
                                         ar.t[0:32, o_state + h * 64:o_state + (h + 1) * 64],
                                         start=(a8 == 0), stop=(a8 == 7)))],
                              reads=["cf0", "state"],
                              claim=[BK[7]] if a8 == 0 else (),
                              publish=[BK[7]] if a8 == 7 else ())
                        if a8 == 7:
                            g8 = h // 8
                            b.op("dve", lambda E, g8=g8, yb=yb:
                                 E.tensor_copy(ar.v(o_y0tm + g8 * 512, 512), psv(yb, 0, 512)),
                                 reads=[BK[yb]], writes=["y0tm"])
                            for qq in range(8):
                                hh = g8 * 8 + qq
                                b.op("dve", lambda E, g8=g8, qq=qq, hh=hh:
                                     E.scalar_tensor_tensor(
                                         ar.v(o_y0tm + g8 * 512 + qq * 64, 64),
                                         psv(7, qq * 64, 64),
                                         ar.t[:, o_expAS + hh:o_expAS + hh + 1],
                                         ar.v(o_y0tm + g8 * 512 + qq * 64, 64),
                                         op0=ALU.mult, op1=ALU.add),
                                     reads=[BK[7], "expAS", "y0tm"], writes=["y0tm"])
                    xw4 = ar.v(o_xwh + gp * 256, 256)
                    kxw = f"xw{gp}"
                    b.op("dve", lambda E, xw4=xw4, h0=h0:
                         E.tensor_tensor(xw4, ar.t[:, o_xstm + h0 * 64:o_xstm + (h0 + 4) * 64],
                                         ar.t[:, o_Wtm + h0:o_Wtm + h0 + 4]
                                         .rearrange("p (h o) -> p h o", o=1)
                                         .broadcast_to([128, 4, 64]), op=ALU.mult),
                         reads=[f"xstm{h0 // 2}", f"xstm{h0 // 2 + 1}", "Wtm"], writes=[kxw])
                    b.mms([lambda E, h0=h0, xw4=xw4:
                           E.matmul(ps_big[0:32, h0 * 64:(h0 + 4) * 64], ar.v(o_btm, 32),
                                    xw4, start=(h0 % 8 == 0), stop=(h0 % 8 == 4))],
                          reads=["btm", kxw],
                          claim=[BK[h0 // 8]] if h0 % 8 == 0 else (),
                          publish=[BK[h0 // 8]] if h0 % 8 == 4 else ())
                for g8 in range(4):
                    stv = ar.v(o_state + g8 * 512, 512, 0, 32)
                    b.op("dve", lambda E, g8=g8, stv=stv:
                         E.tensor_tensor(stv, stv,
                                         ar.t[0:32, o_dbsb + g8 * 8:o_dbsb + (g8 + 1) * 8]
                                         .rearrange("p (h o) -> p h o", o=1)
                                         .broadcast_to([32, 8, 64]), op=ALU.mult),
                         reads=["state", "dbsb0"], writes=["state"])
                    b.op("dve", lambda E, g8=g8, stv=stv:
                         E.tensor_add(stv, stv, ps_big[0:32, g8 * 512:(g8 + 1) * 512]),
                         reads=["state"] + [BK[g8]], writes=["state"])
                if debug and cg == 0:
                    b.dma("pool", d_state[:, :], ar.v(o_state, H * P, 0, 32), reads=["state"])
                b.op("dve", lambda E: E.tensor_mul(ar.v(o_sqt, D), ar.v(o_xstm, D),
                                                   ar.v(o_drepb, D)),
                     reads=["drepb"] + [f"xstm{f}" for f in range(DC)], writes=["lnsq"])
                b.op("dve", lambda E: E.tensor_add(ar.v(o_y0tm, D), ar.v(o_y0tm, D),
                                                   ar.v(o_sqt, D)),
                     reads=["lnsq", "y0tm"], writes=["y0tm"])
                b.dma("pool", y0_s[cg * 128:(cg + 1) * 128, :], ar.v(o_y0tm, D),
                      reads=["y0tm"])
        ar.pop()
        b.barrier_all()

    if "CC" in stages:
        ar.push()
        o_ccs = ar.alloc(H * P)
        if debug:
            b.dma("pool", d_stfin[:, :], ar.v(o_state, H * P, 0, 32), reads=["state"])
        b.op("dve", lambda E: E.tensor_scalar(ar.v(o_ccs, H * P, 0, 32),
                                              ar.v(o_state, H * P, 0, 32),
                                              ar.t[0:32, o_gates + 1:o_gates + 2],
                                              None, op0=ALU.mult),
             reads=["state", "c_gates"], writes=["ccs"])
        b.dma("pool", cc_in[:, :], ar.v(o_ccs, H * P, 0, 32), reads=["ccs"], writes=["ccin"])
        cc_sem = b.cc_sem
        ccin_sem = b.dsem["w:ccin"]
        ccin_cnt = b.dcnt["w:ccin"]

        def _cc(E):
            E.wait_ge(ccin_sem, ccin_cnt)
            E.collective_compute(
                "AllReduce", ALU.add,
                replica_groups=[[0, 1], [2, 3], [4, 5], [6, 7]],
                ins=[cc_in[:, :].opt()], outs=[cc_out[:, :].opt()],
            ).then_inc(cc_sem, 1)
            E.wait_ge(cc_sem, 1)
        b.prog["pool"].append(_cc)
        b.icount["pool"] += 3
        b.dma("pool", ar.v(o_hinit, H * P, 0, 32), cc_out[:, :], writes=["hinit"])
        b.op("dve", lambda E: E.tensor_scalar(ar.v(o_hinit, H * P, 0, 32),
                                              ar.v(o_hinit, H * P, 0, 32),
                                              ar.t[0:32, o_gates:o_gates + 1],
                                              None, op0=ALU.mult),
             reads=["hinit", "c_gates"], writes=["hinit"])
        if debug:
            b.dma("pool", hin_dbg[:, :], ar.v(o_hinit, H * P, 0, 32), reads=["hinit"])
        ar.pop()
        b.barrier_all()

    if "C1" in stages:
        ar.push()
        o_ysbc = ar.alloc(DC * TT_C)
        o_gsbc = ar.alloc(DC * TT_C)
        o_yhc = ar.alloc(DC * TT_C // 2)
        o_ylc = ar.alloc(DC * TT_C // 2)
        o_y0tmc = ar.alloc(2 * D)
        o_x1tmc = ar.alloc(2 * D)
        NPC = 6
        o_pch = ar.alloc(NPC * 512)
        o_pcl = ar.alloc(NPC * 512)
        o_pfC = ar.alloc(1024)
        o_sqtc = ar.alloc(TT_C)
        o_rrowc = ar.alloc(TT_C)
        o_rbc = ar.alloc(TT_C)
        o_cf2c = ar.alloc(32)
        o_cf2sbc = ar.alloc(128)
        o_wtmpc = ar.alloc(TT_C)
        pan_ic = [0]

        def panelC(dram_f, dram_h, dram_l, wk, first, w=1024):
            pi = pan_ic[0] % NPC
            pan_ic[0] += 1
            hi = ar.v(o_pch + pi * 512, (w + 1) // 2).bitcast(BF16)[:, :w]
            lo = ar.v(o_pcl + pi * 512, (w + 1) // 2).bitcast(BF16)[:, :w]
            if first:
                pv = ar.v(o_pfC, w)
                b.dma("sp", pv, dram_f, writes=["pf0"])
                b.op("act", lambda E, hi=hi, pv=pv: E.activation(hi, pv, AF.Copy),
                     reads=["pf0"], writes=[f"ph{pi}"])
                b.op("dve", lambda E, lo=lo, pv=pv, hi=hi:
                     E.tensor_tensor(lo, pv, hi, op=ALU.subtract),
                     reads=["pf0", f"ph{pi}"], writes=[f"pl{pi}"])
                b.dma("pool", dram_h, hi, reads=[f"ph{pi}"], writes=[wk], slot="wch")
                b.dma("pool", dram_l, lo, reads=[f"pl{pi}"], writes=[wk], slot="wcl")
            else:
                b.dma("sp", hi, dram_h, writes=[f"ph{pi}"], reads=[wk])
                b.dma("sp", lo, dram_l, writes=[f"pl{pi}"], reads=[wk])
            return hi, lo, pi

        for ti, (t0, tt) in enumerate(_tiles(T_OWN, TT_C)):
            ntc = tt // 128
            for f in range(DC):
                b.dma("pool", ar.v(o_gsbc + f * TT_C, tt), gs_s[f, :, t0:t0 + tt],
                      writes=[f"gsb{f}"])
            for c in range(ntc):
                b.dma("pool", ar.v(o_y0tmc + c * D, D),
                      y0_s[t0 + c * 128:t0 + c * 128 + 128, :], writes=[f"y0tm{c}"])
                b.dma("pool", ar.v(o_x1tmc + c * D, D),
                      x1_s[T_HALO + t0 + c * 128:T_HALO + t0 + c * 128 + 128, :],
                      writes=[f"x1tm{c}"])
            for cj in range(ntc):
                cg = t0 // 128 + cj
                for h in range(H):
                    a8 = h % 8
                    yb = 4 + ((h // 8) % 2)
                    b.op("dve", lambda E, h=h, cg=cg:
                         E.tensor_scalar(ar.v(o_cf2c, 32), ar.v(o_ctm + cg * 32, 32),
                                         ar.t[:, o_ftm + cg * 32 + h:o_ftm + cg * 32 + h + 1],
                                         None, op0=ALU.mult),
                         reads=["ctm", "ftm"], writes=["cf2"])
                    emit_tp(ar.v(o_cf2c, 32), ar.v(o_cf2sbc, 128, 0, 32),
                            "cf2", "cf2sb", 7, 0, 128, 32)
                    b.mms([lambda E, h=h, yb=yb, a8=a8:
                           E.matmul(psv(yb, a8 * 64, 64), ar.v(o_cf2sbc, 128, 0, 32),
                                    ar.t[0:32, o_hinit + h * 64:o_hinit + (h + 1) * 64],
                                    start=(a8 == 0), stop=(a8 == 7))],
                          reads=["hinit", "cf2sb"],
                          claim=[BK[yb]] if a8 == 0 else (),
                          publish=[BK[yb]] if a8 == 7 else ())
                    if a8 == 7:
                        g8 = h // 8
                        yv = ar.v(o_y0tmc + cj * D + g8 * 512, 512)
                        b.op("dve", lambda E, yv=yv, yb=yb:
                             E.tensor_add(yv, yv, psv(yb, 0, 512)),
                             reads=[BK[yb], f"y0tm{cj}"], writes=[f"y0tm{cj}"])
                for f in range(DC):
                    emit_tp(ar.v(o_y0tmc + cj * D + f * 128, 128),
                            ar.v(o_ysbc + f * TT_C + cj * 128, 128),
                            f"y0tm{cj}", f"ysb{f}", 7, 0, 128, 128)
                for f in range(DC):
                    yv = ar.v(o_ysbc + f * TT_C + cj * 128, 128)
                    gv = ar.v(o_gsbc + f * TT_C + cj * 128, 128)
                    b.op("dve", lambda E, yv=yv, gv=gv:
                         E.tensor_mul(yv, yv, gv),
                         reads=[f"gsb{f}", f"ysb{f}"], writes=[f"ysb{f}"])
            for f in range(DC):
                sq = ar.v(o_sqtc, tt)
                b.op("act", lambda E, f=f, sq=sq, tt=tt:
                     E.activation(sq, ar.v(o_ysbc + f * TT_C, tt), AF.Square),
                     reads=[f"ysb{f}"], writes=["sqt"])
                b.mms([lambda E, sq=sq, f=f, tt=tt:
                       E.matmul(psv(6, 0, tt, 0, 1), ones_col(), sq,
                                start=(f == 0), stop=(f == DC - 1))],
                      reads=["sqt", "c_ones"],
                      claim=[BK[6]] if f == 0 else (),
                      publish=[BK[6]] if f == DC - 1 else ())
            b.op("act", lambda E, tt=tt: E.activation(ar.t[0:1, o_rrowc:o_rrowc + tt],
                                                      psv(6, 0, tt, 0, 1), AF.Sqrt,
                                                      scale=1.0 / D, bias=EPS_RMS),
                 reads=[BK[6], "c_eps"], writes=["rrow"])
            b.op("dve", lambda E, tt=tt: E.reciprocal(ar.t[0:1, o_rrowc:o_rrowc + tt],
                                                      ar.t[0:1, o_rrowc:o_rrowc + tt]),
                 reads=["rrow"], writes=["rrow"])
            b.mms([lambda E, tt=tt: E.matmul(psv(7, 0, tt), ones_lhsT(128),
                                             ar.t[0:1, o_rrowc:o_rrowc + tt], start=True, stop=True)],
                  reads=["rrow", "c_ones"], writes=[BK[7]])
            b.op("dve", lambda E, tt=tt: E.tensor_copy(ar.v(o_rbc, tt), psv(7, 0, tt)),
                 reads=[BK[7]], writes=["rb"])
            for f in range(DC):
                yv = ar.v(o_ysbc + f * TT_C, tt)
                b.op("dve", lambda E, yv=yv, f=f, tt=tt:
                     E.scalar_tensor_tensor(yv, yv, ar.t[:, o_normw + f:o_normw + f + 1],
                                            ar.v(o_rbc, tt), op0=ALU.mult, op1=ALU.mult),
                     reads=["rb", "c_normw", f"ysb{f}"], writes=[f"ysb{f}"])
                hiv = ar.v(o_yhc + f * TT_C // 2, TT_C // 2).bitcast(BF16)[:, :tt]
                lov = ar.v(o_ylc + f * TT_C // 2, TT_C // 2).bitcast(BF16)[:, :tt]
                b.op("act", lambda E, hiv=hiv, yv=yv: E.activation(hiv, yv, AF.Copy),
                     reads=[f"ysb{f}"], writes=[f"yh{f}"])
                b.op("dve", lambda E, lov=lov, yv=yv, hiv=hiv:
                     E.tensor_tensor(lov, yv, hiv, op=ALU.subtract),
                     reads=[f"ysb{f}", f"yh{f}"], writes=[f"yl{f}"])
            for mg in range(2):
                for k in range(DC):
                    whi, wlo, pi = panelC(
                        W_out[k * 128:(k + 1) * 128, mg * 1024:(mg + 1) * 1024],
                        W_out_h[k * 128:(k + 1) * 128, mg * 1024:(mg + 1) * 1024],
                        W_out_l[k * 128:(k + 1) * 128, mg * 1024:(mg + 1) * 1024],
                        "splitWout%d_%d" % (k, mg), ti == 0)
                    yhiv = ar.v(o_yhc + k * TT_C // 2, TT_C // 2).bitcast(BF16)[:, :tt]
                    ylov = ar.v(o_ylc + k * TT_C // 2, TT_C // 2).bitcast(BF16)[:, :tt]
                    def wfns(j0, j1, k=k, tt=tt, whi=whi, wlo=wlo, yhiv=yhiv, ylov=ylov):
                        fns = []
                        for j in range(j0, j1):
                            fns += [
                                (lambda E, j=j, tt=tt, whi=whi, yhiv=yhiv, k=k:
                                 E.matmul(psv(j, 0, tt), whi[:, j * 128:(j + 1) * 128], yhiv,
                                          start=(k == 0), stop=False)),
                                (lambda E, j=j, tt=tt, whi=whi, ylov=ylov:
                                 E.matmul(psv(j, 0, tt), whi[:, j * 128:(j + 1) * 128], ylov,
                                          start=False, stop=False)),
                                (lambda E, j=j, tt=tt, wlo=wlo, yhiv=yhiv, k=k:
                                 E.matmul(psv(j, 0, tt), wlo[:, j * 128:(j + 1) * 128], yhiv,
                                          start=False, stop=(k == DC - 1))),
                            ]
                        return fns
                    if k == 0:
                        b.mms(wfns(0, 4), reads=[f"ph{pi}", f"pl{pi}", f"yh{k}", f"yl{k}"],
                              claim=BK[:4])
                        b.mms(wfns(4, 8), reads=[f"ph{pi}", f"pl{pi}", f"yh{k}", f"yl{k}"],
                              claim=BK[4:8])
                    else:
                        b.mms(wfns(0, 8), reads=[f"ph{pi}", f"pl{pi}", f"yh{k}", f"yl{k}"],
                              publish=BK if k == DC - 1 else ())
                for j in range(8):
                    f = mg * 8 + j
                    wv = ar.v(o_wtmpc, tt)
                    b.op("act", lambda E, wv=wv, j=j, tt=tt:
                         E.activation(wv, psv(j, 0, tt), AF.Copy),
                         reads=[BK[j]], writes=["wtmp"])
                    for c in range(ntc):
                        b.op("pe", lambda E, c=c, j=j:
                             E.transpose(psv(j, 0, 128), ar.v(o_wtmpc + c * 128, 128), ident),
                             reads=["wtmp", "c_ident"], writes=[BK[j]])
                        xsl = ar.v(o_x1tmc + c * D + f * 128, 128)
                        b.op("dve", lambda E, xsl=xsl, j=j:
                             E.tensor_add(xsl, xsl, psv(j, 0, 128)),
                             reads=[BK[j], f"x1tm{c}"], writes=[f"x1tm{c}"])
            for c in range(ntc):
                b.dma("pool", x2_s[t0 + c * 128:t0 + c * 128 + 128, :],
                      ar.v(o_x1tmc + c * D, D), reads=[f"x1tm{c}"])
        ar.pop()
        b.barrier_all()

    if "C2" in stages:
        emit_ffn_stage(T_OWN,
                       lambda t, n: x2_s[t:t + n, :],
                       lambda t, n: out_t[t:t + n, :], False)
        b.barrier_all()

    with nc.Block() as block:
        b.finalize(block)
    return nc, es


def _pad_cols(v, n):
    out = np.zeros(n, np.float32)
    out[:v.shape[0]] = v
    return out


def make_in_maps(inputs):
    x = np.ascontiguousarray(np.asarray(inputs["L0_attn_x"], np.float32))
    W_in = np.ascontiguousarray(np.asarray(inputs["L0_attn_W_in"], np.float32))
    b_in = np.asarray(inputs["L0_attn_b_in"], np.float32)
    conv_w = np.asarray(inputs["L0_attn_conv_w"], np.float32)
    conv_b = np.asarray(inputs["L0_attn_conv_b"], np.float32)
    A_log = np.asarray(inputs["L0_attn_A_log"], np.float32)
    dt_bias = np.asarray(inputs["L0_attn_dt_bias"], np.float32)
    D_param = np.asarray(inputs["L0_attn_D_param"], np.float32)
    norm_w = np.asarray(inputs["L0_attn_norm_w"], np.float32)
    W_out = np.ascontiguousarray(np.asarray(inputs["L0_attn_W_out"], np.float32))
    wgu = np.asarray(inputs["L0_ffn_w_gate_up"], np.float32)
    wd = np.asarray(inputs["L0_ffn_w_down"], np.float32)

    WguT = np.ascontiguousarray(wgu.T)
    WdT = np.ascontiguousarray(wd.T)

    b_in_t = np.ascontiguousarray(_pad_cols(b_in, 33 * 128).reshape(33, 128).T)
    cwp = np.zeros((NCC * 128, 4), np.float32)
    cwp[:CONV] = conv_w
    conv_w_t = np.ascontiguousarray(
        cwp.reshape(NCC, 128, 4).transpose(1, 0, 2).reshape(128, NCC * 4))
    conv_b_t = np.ascontiguousarray(_pad_cols(conv_b, NCC * 128).reshape(NCC, 128).T)
    d_rep_t = np.ascontiguousarray(np.repeat(D_param, P).reshape(DC, 128).T)
    norm_w_t = np.ascontiguousarray(norm_w.reshape(DC, 128).T)

    cons = np.zeros((128, 3 * 128 + 3), np.float32)
    cons[:, 0:128] = np.eye(128, dtype=np.float32)
    cons[:, 128:256] = np.triu(np.ones((128, 128), np.float32))
    cons[:, 256:384] = np.where(np.triu(np.ones((128, 128), np.float32)) > 0, 0.0, MASK_NEG)
    cons[:, 384] = 1.0
    cons[:, 385] = EPS_LN
    cons[:, 386] = EPS_RMS

    B, L, _ = x.shape
    Th = L // 2
    in_maps = []
    for core in range(8):
        bb, half = core // 2, core % 2
        own = x[bb, half * Th:(half + 1) * Th]
        halo = x[bb, Th - T_HALO:Th] if half == 1 else np.zeros((T_HALO, D), np.float32)
        x_pre_a = np.ascontiguousarray(np.concatenate([halo, own], 0))
        g = np.zeros((128, 2), np.float32)
        g[:, 0] = 1.0 if half == 1 else 0.0
        g[:, 1] = 1.0 if half == 0 else 0.0
        in_maps.append({
            "x_pre": x_pre_a, "WguT": WguT, "WdT": WdT, "W_in": W_in, "W_out": W_out,
            "b_in_t": b_in_t, "conv_w_t": conv_w_t, "conv_b_t": conv_b_t,
            "d_rep_t": d_rep_t, "d_rep_row": np.ascontiguousarray(np.repeat(D_param, P).reshape(1, D)),
            "norm_w_t": norm_w_t,
            "a_log": np.ascontiguousarray(A_log.reshape(32, 1)),
            "dt_bias_t": np.ascontiguousarray(dt_bias.reshape(32, 1)),
            "consts": cons, "gates_t": g,
        })
    return in_maps


def kernel(**inputs):
    in_maps = make_in_maps(inputs)
    nc, es = build_program(debug=False)
    try:
        res = run_bass_kernel_spmd(nc, in_maps, core_ids=list(range(8)))
    finally:
        es.close()
    x = np.asarray(inputs["L0_attn_x"])
    B, L, _ = x.shape
    Th = L // 2
    out = np.empty((B, L, D), np.float32)
    for core in range(8):
        bb, half = core // 2, core % 2
        out[bb, half * Th:(half + 1) * Th] = res.results[core]["out"]
    return out
